# version 05
# baseline (speedup 1.0000x reference)
import os
import numpy as np
import ml_dtypes
from contextlib import ExitStack

import concourse.bass as bass
import concourse.mybir as mybir
from concourse.bass_utils import run_bass_kernel_spmd

F32 = mybir.dt.float32
BF16 = mybir.dt.bfloat16
I32 = mybir.dt.int32
AX = mybir.AxisListType.X
OP = mybir.AluOpType
AF = mybir.ActivationFunctionType

NCORES = 8
T = 8192
D = 2048
TOK = T // NCORES
NT = TOK // 128
H = 16
NOPE, ROPE, VD = 128, 64, 128
QHD = NOPE + ROPE
SCALE = QHD ** -0.5
DQA = 1536
DCKV = 512
DKVA = DCKV + ROPE
NE, DI = 32, 768
NCH, DIC = 16, 128
SMALL_SCALE = 0.05
EPS_LN = 1e-5
EPS_RMS = 1e-6
NEGBIG = -1e30

R_QN = 0
R_QPE = 2048
R_KN = 3072
R_KPE = 5120
R_VF = 5184
FMROWS = 5184 + 2048

ENGINES = ("tensor", "vector", "scalar", "gpsimd", "sync")


class Sem:
    __slots__ = ("h", "n")

    def __init__(self, stack, nc, name):
        self.h = stack.enter_context(nc.semaphore(name))
        self.n = 0


class Build:
    def __init__(self):
        self.nc = bass.Bass("TRN2", target_bir_lowering=False, detect_race_conditions=False)
        self.root = ExitStack()
        self.q = {e: [] for e in ENGINES}
        self.dbg = set(x for x in os.environ.get("MK_DEBUG_OUT", "").split(",") if x)
        if "att" in self.dbg:
            self.dbg |= {"att_gq", "att_gk", "att_gqp", "att_gkp", "att_gvf", "att_afm"}
            self.dbg |= {"att_%s%d" % (n, q) for q in (0, 5) for n in
                         ("p", "negm", "ssum", "alph", "nrow", "rcp", "scr", "qnh", "knh", "vh")}
        self.dbg_outs = {}

    def on(self, eng, fn, inc=None, k=1):
        if inc is not None:
            inc.n += k
            def wrapped(e, fn=fn, inc=inc, k=k):
                r = fn(e)
                r.then_inc(inc.h, k)
                return r
            self.q[eng].append(wrapped)
            return inc.n
        self.q[eng].append(fn)
        return None

    def last_inc(self, eng, sem, k=1):
        sem.n += k
        idx = len(self.q[eng]) - 1
        while getattr(self.q[eng][idx], "is_wait", False):
            idx -= 1
        prev = self.q[eng][idx]
        def wrapped(e, prev=prev, sem=sem, k=k):
            r = prev(e)
            r.then_inc(sem.h, k)
            return r
        self.q[eng][idx] = wrapped
        return sem.n

    def wait(self, eng, sem, val):
        if val > 0:
            f = lambda e, s=sem, v=val: e.wait_ge(s.h, v)
            f.is_wait = True
            self.q[eng].append(f)

    def dma(self, eng, out_ap, in_ap, sem):
        return self.on(eng, lambda e, o=out_ap, i=in_ap: e.dma_start(o, i),
                       inc=sem, k=16)

    def barrier(self):
        nc = self.nc
        with nc.Block() as blk:
            for name in ENGINES:
                ops = self.q[name]
                if not ops:
                    continue
                def runner(e, ops=ops):
                    for f in ops:
                        f(e)
                getattr(blk, name)(runner)
        self.q = {e: [] for e in ENGINES}

    def sbuf(self, stack, name, shape, dt, side="left"):
        return stack.enter_context(self.nc.sbuf_tensor(name, shape, dt, side=side))

    def psum(self, stack, name, shape, dt):
        return stack.enter_context(self.nc.psum_tensor(name, shape, dt))

    def dbg_dump(self, name, src_ap, shape, dt, sem, waits=()):
        if name not in self.dbg:
            return
        h = self.nc.dram_tensor("dbg_" + name, shape, dt, kind="ExternalOutput")
        self.dbg_outs[name] = h
        for wsem in waits:
            self.wait("sync", wsem, wsem.n)
        self.dma("sync", h.ap(), src_ap, sem)


def build_program():
    B = Build()
    nc = B.nc
    root = B.root
    PH = int(os.environ.get("MK_PHASES", "99"))

    dram_in = lambda n, s, dt=F32: nc.dram_tensor(n, s, dt, kind="ExternalInput")
    dram_out = lambda n, s, dt=F32: nc.dram_tensor(n, s, dt, kind="ExternalOutput")

    x_in = dram_in("x_own", [TOK, D])
    qaw_t = dram_in("qaw_t", [D, DQA])
    qbw_t = dram_in("qbw_t", [DQA, H * QHD])
    qa_g = dram_in("qa_gamma", [128, DQA // 128])
    kvaw_t = dram_in("kvaw_t", [D, DKVA])
    kvbw_t = dram_in("kvbw_t", [DCKV, H * (NOPE + VD)])
    kva_g = dram_in("kva_gamma", [128, DCKV // 128])
    ow_t = dram_in("ow_t", [H * VD, D])
    gate_t = dram_in("gate_t", [D, NE])
    wg_w = dram_in("Wg", [NE, D, DI])
    wu_w = dram_in("Wu", [NE, D, DI])
    wd_w = dram_in("Wd", [NE, DI, D])
    cwg_w = dram_in("cWg", [NCH, D, DIC])
    cwu_w = dram_in("cWu", [NCH, D, DIC])
    cwd_w = dram_in("cWd", [NCH, DIC, D])
    ident_in = dram_in("ident_c", [128, 128], BF16)
    mask_in = dram_in("masktile", [128, 128])
    cosfm_in = dram_in("cosfm_c", [128, TOK])
    sinfm_in = dram_in("sinfm_c", [128, TOK])
    costm_in = dram_in("costm_c", [TOK, ROPE])
    sintm_in = dram_in("sintm_c", [TOK, ROPE])
    riota_in = dram_in("riota_c", [128, NE])
    epsc_in = dram_in("epsc", [128, 2])
    rotm_in = dram_in("rotm", [128, 128], BF16)
    cid_in = dram_in("cid", [1, 1], I32)

    y_out = dram_out("y", [TOK, D])

    fmb = nc.dram_tensor("fmb", [FMROWS, TOK], BF16)
    gfm = nc.dram_tensor("gfm", [NCORES * FMROWS, TOK], BF16)
    abn = nc.dram_tensor("abn", [2 * VD, T], BF16)
    gattn = nc.dram_tensor("gattn", [H * VD, T], BF16)

    x_tm = B.sbuf(root, "x_tm", [128, NT, D], F32)
    ident = B.sbuf(root, "ident", [128, 128], BF16)
    mask = B.sbuf(root, "mask", [128, 128], F32)
    sq_scr = B.sbuf(root, "sq_scr", [128, D], BF16)
    st_s = B.sbuf(root, "st_s", [128, NT], F32)
    st_q = B.sbuf(root, "st_q", [128, NT], F32)
    st_m = B.sbuf(root, "st_m", [128, NT], F32)
    st_r = B.sbuf(root, "st_r", [128, NT], F32)
    cid_s = B.sbuf(root, "cid_s", [1, 1], I32)
    eps_s = B.sbuf(root, "eps_s", [128, 2], F32)

    s_ld = Sem(root, nc, "s_ld")
    s_mm = Sem(root, nc, "s_mm")
    s_vx = Sem(root, nc, "s_vx")
    s_sc = Sem(root, nc, "s_sc")
    s_bn = Sem(root, nc, "s_bn")
    s_wl = Sem(root, nc, "s_wl")
    s_fr = Sem(root, nc, "s_fr")
    s_fd = Sem(root, nc, "s_fd")
    s_cc = Sem(root, nc, "s_cc")
    s_out = Sem(root, nc, "s_out")

    on, wait, dma = B.on, B.wait, B.dma

    def vbar():
        return on("vector", lambda e: e.engine_nop(), inc=s_vx)

    def tbar():
        return B.last_inc("tensor", s_mm)

    early_fm = ExitStack()
    ph1 = ExitStack()
    early_tm = ExitStack()

    cosfm = B.sbuf(early_fm, "cosfm", [128, TOK], F32)
    sinfm = B.sbuf(early_fm, "sinfm", [128, TOK], F32)
    qa_gs = B.sbuf(early_fm, "qa_gs", [128, DQA // 128], F32)
    kva_gs = B.sbuf(early_fm, "kva_gs", [128, DCKV // 128], F32)
    xln1_fm = B.sbuf(ph1, "xln1_fm", [128, D // 128, TOK], BF16)
    costm = B.sbuf(early_tm, "costm", [128, NT, ROPE], F32, side="right")
    sintm = B.sbuf(early_tm, "sintm", [128, NT, ROPE], F32, side="right")

    dma("sync", ident[:, :], ident_in.ap(), s_ld)
    dma("sync", mask[:, :], mask_in.ap(), s_ld)
    dma("sync", cosfm[:, :], cosfm_in.ap(), s_ld)
    dma("sync", sinfm[:, :], sinfm_in.ap(), s_ld)
    dma("sync", costm[:, :, :], costm_in.ap().rearrange("(n p) r -> p n r", p=128), s_ld)
    dma("sync", sintm[:, :, :], sintm_in.ap().rearrange("(n p) r -> p n r", p=128), s_ld)
    dma("sync", qa_gs[:, :], qa_g.ap(), s_ld)
    dma("sync", kva_gs[:, :], kva_g.ap(), s_ld)
    dma("sync", cid_s[:, :], cid_in.ap(), s_ld)
    dma("sync", eps_s[:, :], epsc_in.ap(), s_ld)
    dma("sync", x_tm[:, :, :], x_in.ap().rearrange("(n p) d -> p n d", p=128), s_ld)
    v_x = s_ld.n

    def emit_norm_stats(src_fn, n_tiles, width):
        for t in range(n_tiles):
            src = src_fn(t)
            if t > 0:
                wait("scalar", s_sc, s_sc.n)
            on("scalar", lambda e, s=src, t=t: e.activation(
                sq_scr[:, 0:width], s, AF.Square, accum_out=st_q[:, t:t+1]), inc=s_sc)
            on("vector", lambda e, s=src, t=t: e.reduce_sum(
                st_s[:, t:t+1], s, axis=AX))
        v_s = s_sc.n
        vv = vbar()
        wait("vector", s_vx, vv)
        wait("vector", s_sc, v_s)
        on("vector", lambda e: e.tensor_scalar(
            st_m[:, 0:n_tiles], st_s[:, 0:n_tiles], 1.0 / width, None, OP.mult))
        wait("vector", s_vx, vbar())
        on("vector", lambda e: e.tensor_tensor(
            st_s[:, 0:n_tiles], st_m[:, 0:n_tiles], st_m[:, 0:n_tiles], OP.mult))
        wait("vector", s_vx, vbar())
        on("vector", lambda e: e.scalar_tensor_tensor(
            st_q[:, 0:n_tiles], st_q[:, 0:n_tiles], 1.0 / width,
            st_s[:, 0:n_tiles], OP.mult, OP.subtract))
        vv2 = vbar()
        wait("scalar", s_vx, vv2)
        v_s2 = on("scalar", lambda e: e.activation(
            st_q[:, 0:n_tiles], st_q[:, 0:n_tiles], AF.Sqrt, bias=eps_s[:, 0:1]), inc=s_sc)
        wait("vector", s_sc, v_s2)
        on("vector", lambda e: e.reciprocal(st_r[:, 0:n_tiles], st_q[:, 0:n_tiles]))
        return vbar()

    def emit_rms_tile(t, width, src_psum):
        wait("scalar", s_sc, s_sc.n)
        on("scalar", lambda e, t=t, s=src_psum: e.activation(
            sq_scr[:, 0:width], s, AF.Square, accum_out=st_q[:, t:t+1]), inc=s_sc)
        v0 = s_sc.n
        wait("scalar", s_sc, v0)
        sv = on("scalar", lambda e, t=t: e.activation(
            st_q[:, t:t+1], st_q[:, t:t+1], AF.Sqrt, bias=eps_s[:, 1:2], scale=1.0 / width), inc=s_sc)
        wait("vector", s_sc, sv)
        on("vector", lambda e, t=t: e.reciprocal(st_r[:, t:t+1], st_q[:, t:t+1]))
        vv = vbar()
        wait("vector", s_vx, vv)

    def emit_transposes(pairs, tr_ps, gamma=None, scale=None, wait_sem=None,
                        wait_val=None):
        if wait_val:
            wait("tensor", wait_sem, wait_val)
        evac = [0, 0]
        for k, (src, dst, c) in enumerate(pairs):
            pp = tr_ps[k % 2]
            wait("tensor", s_vx, evac[k % 2])
            on("tensor", lambda e, s=src, pp=pp: e.transpose(pp[:, 0:128], s, ident[:, :]))
            mv = tbar()
            wait("vector", s_mm, mv)
            if gamma is not None:
                on("vector", lambda e, d=dst, pp=pp, c=c: e.tensor_scalar(
                    d, pp[:, 0:128], gamma[:, c:c+1], None, OP.mult))
            elif scale is not None:
                on("vector", lambda e, d=dst, pp=pp: e.tensor_scalar(
                    d, pp[:, 0:128], float(scale), None, OP.mult))
            else:
                on("vector", lambda e, d=dst, pp=pp: e.tensor_copy(d, pp[:, 0:128]))
            evac[k % 2] = vbar()
        return s_vx.n

    def finish():
        dma("sync", y_out.ap().rearrange("(n p) d -> p n d", p=128), x_tm[:, :, :], s_out)
        wait("sync", s_out, s_out.n)
        B.barrier()
        return nc, B.dbg_outs

    ph1b = ExitStack()
    ln_scr = B.sbuf(ph1b, "ln_scr", [128, NT, D], BF16)
    tr_ps1 = [B.psum(ph1b, "tr1_%d" % i, [128, 1024], BF16) for i in range(2)]

    wait("scalar", s_ld, v_x)
    wait("vector", s_ld, v_x)
    wait("tensor", s_ld, v_x)
    v_stat = emit_norm_stats(lambda t: x_tm[:, t, :], NT, D)
    wait("vector", s_vx, v_stat)
    for t in range(NT):
        on("vector", lambda e, t=t: e.tensor_scalar(
            ln_scr[:, t, :], x_tm[:, t, :], st_m[:, t:t+1], st_r[:, t:t+1],
            OP.subtract, OP.mult))
    v_ln1 = vbar()
    pairs = [(ln_scr[:, t, c*128:(c+1)*128], xln1_fm[:, c, t*128:(t+1)*128], c)
             for t in range(NT) for c in range(D // 128)]
    B.dbg_dump("ln_tm", ln_scr[:, :, :], [128, NT, D], BF16, s_out, waits=(s_vx, s_sc))
    B.dbg_dump("stats_m", st_m[:, :], [128, NT], F32, s_out, waits=(s_vx, s_sc))
    B.dbg_dump("stats_r", st_r[:, :], [128, NT], F32, s_out, waits=(s_vx, s_sc))
    emit_transposes(pairs, tr_ps1, wait_sem=s_vx, wait_val=v_ln1)
    B.dbg_dump("xln1_fm", xln1_fm[:, :, :], [128, D // 128, TOK], BF16, s_out, waits=(s_vx, s_sc))
    B.barrier()
    ph1b.close()

    if PH < 2:
        return finish()

    ph2k = ExitStack()
    ckvn_fm = B.sbuf(ph2k, "ckvn_fm", [128, DCKV // 128, TOK], BF16)
    kpe_fm = B.sbuf(ph2k, "kpe_fm", [64, TOK], BF16)

    ph2a = ExitStack()
    kvaw_s = B.sbuf(ph2a, "kvaw_s", [128, D // 128, DKVA], BF16)
    for c in range(D // 128):
        dma("gpsimd", kvaw_s[:, c, :], kvaw_t.ap()[c*128:(c+1)*128, :], s_wl)
    v_wkva = s_wl.n
    ckv_ps = [B.psum(ph2a, "ckv_ps%d" % i, [128, 1024], F32) for i in range(2)]
    tr_ps2 = [B.psum(ph2a, "tr2_%d" % i, [128, 1024], BF16) for i in range(2)]
    ckvn_tm = B.sbuf(ph2a, "ckvn_tm", [128, NT, DCKV], BF16)
    kpe_tm = B.sbuf(ph2a, "kpe_tm", [128, NT, ROPE], F32)
    kpe_rot = B.sbuf(ph2a, "kpe_rot", [128, NT, ROPE], BF16)
    rop_a = B.sbuf(ph2a, "rop_a", [128, ROPE], F32)
    rop_b = B.sbuf(ph2a, "rop_b", [128, ROPE], F32)

    wait("tensor", s_wl, v_wkva)
    evac_v = [0, 0]
    for t in range(NT):
        pp = ckv_ps[t % 2]
        wait("tensor", s_vx, evac_v[t % 2])
        for (lo, hi) in ((0, 512), (512, DKVA)):
            for c in range(D // 128):
                on("tensor", lambda e, t=t, c=c, lo=lo, hi=hi, pp=pp: e.matmul(
                    pp[:, lo:hi], xln1_fm[:, c, t*128:(t+1)*128],
                    kvaw_s[:, c, lo:hi],
                    start=(c == 0), stop=(c == D // 128 - 1)))
        mv = tbar()
        wait("scalar", s_mm, mv)
        emit_rms_tile(t, DCKV, pp[:, 0:DCKV])
        on("vector", lambda e, t=t, pp=pp: e.tensor_scalar(
            ckvn_tm[:, t, :], pp[:, 0:DCKV], st_r[:, t:t+1], None, OP.mult))
        on("vector", lambda e, t=t, pp=pp: e.tensor_copy(
            kpe_tm[:, t, :], pp[:, DCKV:DKVA]))
        HR = ROPE // 2
        def rope_tm1(e, t=t):
            kt = kpe_tm[:, t, :]
            e.tensor_tensor(rop_a[:, 0:HR], kt[:, HR:ROPE], sintm[:, t, 0:HR], OP.mult)
            e.tensor_tensor(rop_a[:, HR:ROPE], kt[:, 0:HR], sintm[:, t, HR:ROPE], OP.mult)
            e.tensor_tensor(rop_b[:, 0:ROPE], kt[:, 0:ROPE], costm[:, t, 0:ROPE], OP.mult)
        on("vector", rope_tm1)
        wait("vector", s_vx, vbar())
        on("vector", lambda e, t=t: e.tensor_tensor(
            kpe_rot[:, t, :], rop_b[:, 0:ROPE], rop_a[:, 0:ROPE], OP.add))
        evac_v[t % 2] = vbar()
    v_ckv_done = s_vx.n

    pairs = [(ckvn_tm[:, t, c*128:(c+1)*128], ckvn_fm[:, c, t*128:(t+1)*128], c)
             for t in range(NT) for c in range(DCKV // 128)]
    emit_transposes(pairs, tr_ps2, gamma=kva_gs, wait_sem=s_vx, wait_val=v_ckv_done)
    evac = [s_vx.n, s_vx.n]
    for t in range(NT):
        pp = tr_ps2[t % 2]
        wait("tensor", s_vx, evac[t % 2])
        on("tensor", lambda e, t=t, pp=pp: e.transpose(
            pp[0:ROPE, 0:128], kpe_rot[:, t, :], ident[:, :]))
        mv = tbar()
        wait("vector", s_mm, mv)
        on("vector", lambda e, t=t, pp=pp: e.tensor_copy(
            kpe_fm[0:ROPE, t*128:(t+1)*128], pp[0:ROPE, 0:128]))
        evac[t % 2] = vbar()
    B.dbg_dump("ckvn_fm", ckvn_fm[:, :, :], [128, DCKV // 128, TOK], BF16, s_out, waits=(s_vx, s_sc))
    B.dbg_dump("kpe_fm", kpe_fm[0:64, :], [64, TOK], BF16, s_out, waits=(s_vx, s_sc))
    B.barrier()
    ph2a.close()
    if int(os.environ.get("MK_P2", "9")) < 3:
        return finish()

    ph2b = ExitStack()
    kvbw_s = B.sbuf(ph2b, "kvbw_s", [128, DCKV // 128, H * (NOPE + VD)], BF16)
    for c in range(DCKV // 128):
        dma("gpsimd", kvbw_s[:, c, :], kvbw_t.ap()[c*128:(c+1)*128, :], s_wl)
    v_wkvb = s_wl.n
    kn_hd = B.sbuf(ph2b, "kn_hd", [128, 2, TOK], BF16)
    kn_ps = [B.psum(ph2b, "kn_ps%d" % i, [128, 512], F32) for i in range(2)]
    wait("tensor", s_wl, v_wkvb)
    kn_dma_v = [0, 0]
    evac = [0, 0]
    k = 0
    for f in range(2 * H):
        slot = f % 2
        wait("vector", s_bn, kn_dma_v[slot])
        for tb in range(TOK // 512):
            pp = kn_ps[k % 2]
            wait("tensor", s_vx, evac[k % 2])
            for c in range(DCKV // 128):
                on("tensor", lambda e, f=f, tb=tb, c=c, pp=pp: e.matmul(
                    pp[:, :], kvbw_s[:, c, f*128:(f+1)*128],
                    ckvn_fm[:, c, tb*512:(tb+1)*512],
                    start=(c == 0), stop=(c == DCKV // 128 - 1)))
            mv = tbar()
            wait("vector", s_mm, mv)
            on("vector", lambda e, tb=tb, pp=pp, slot=slot: e.tensor_copy(
                kn_hd[:, slot, tb*512:(tb+1)*512], pp[:, :]))
            evac[k % 2] = vbar()
            k += 1
        wait("sync", s_vx, s_vx.n)
        row = R_KN + f * 128 if f < H else R_VF + (f - H) * 128
        kn_dma_v[slot] = dma(
            "sync", fmb.ap()[row: row + 128, :], kn_hd[:, slot, :], s_bn)
    wait("sync", s_vx, s_vx.n)
    dma("sync", fmb.ap()[R_KPE:R_KPE + ROPE, :], kpe_fm[0:ROPE, :], s_bn)
    B.barrier()
    ph2b.close()
    ph2k.close()
    early_tm.close()

    if PH < 3:
        return finish()

    qan_fm = B.sbuf(early_tm, "qan_fm", [128, DQA // 128, TOK], BF16, side="right")
    ph3a = ExitStack()
    qaw_s = B.sbuf(ph3a, "qaw_s", [128, D // 128, DQA], BF16)
    for c in range(D // 128):
        dma("gpsimd", qaw_s[:, c, :], qaw_t.ap()[c*128:(c+1)*128, :], s_wl)
    v_wqa = s_wl.n
    qa_ps = [B.psum(ph3a, "qa_ps%d" % i, [128, 1536], F32) for i in range(2)]
    tr_ps3 = [B.psum(ph3a, "tr3_%d" % i, [128, 1024], BF16) for i in range(2)]
    qan_tm = B.sbuf(ph3a, "qan_tm", [128, NT, DQA], BF16)
    wait("tensor", s_wl, v_wqa)
    evac_v = [0, 0]
    for t in range(NT):
        pp = qa_ps[t % 2]
        wait("tensor", s_vx, evac_v[t % 2])
        for c in range(D // 128):
            for nb in range(DQA // 512):
                on("tensor", lambda e, t=t, c=c, nb=nb, pp=pp: e.matmul(
                    pp[:, nb*512:(nb+1)*512], xln1_fm[:, c, t*128:(t+1)*128],
                    qaw_s[:, c, nb*512:(nb+1)*512],
                    start=(c == 0), stop=(c == D // 128 - 1)))
        mv = tbar()
        wait("scalar", s_mm, mv)
        emit_rms_tile(t, DQA, pp[:, :])
        on("vector", lambda e, t=t, pp=pp: e.tensor_scalar(
            qan_tm[:, t, :], pp[:, :], st_r[:, t:t+1], None, OP.mult))
        evac_v[t % 2] = vbar()
    v_qan = s_vx.n
    pairs = [(qan_tm[:, t, c*128:(c+1)*128], qan_fm[:, c, t*128:(t+1)*128], c)
             for t in range(NT) for c in range(DQA // 128)]
    emit_transposes(pairs, tr_ps3, gamma=qa_gs, wait_sem=s_vx, wait_val=v_qan)
    B.dbg_dump("qan_fm", qan_fm[:, :, :], [128, DQA // 128, TOK], BF16, s_out, waits=(s_vx, s_sc))
    B.barrier()
    ph3a.close()

    ph3b = ExitStack()
    NFQ = 6
    NSL = (H * QHD // 128) // NFQ
    qbw_sl = B.sbuf(ph3b, "qbw_sl", [128, 2, DQA // 128, NFQ * 128], BF16)
    qh_sb = B.sbuf(ph3b, "qh_sb", [128, 4, TOK], BF16)
    qpe_bf = B.sbuf(ph3b, "qpe_bf", [128, TOK], BF16)
    rope_s1 = B.sbuf(ph3b, "rope_s1", [128, TOK], F32)
    rope_s2 = B.sbuf(ph3b, "rope_s2", [128, TOK], F32)
    rotm_s = B.sbuf(ph3b, "rotm_s", [128, 128], BF16)
    q_ps = [B.psum(ph3b, "q_ps%d" % i, [128, 512], F32) for i in range(2)]
    rot_ps = B.psum(ph3b, "rot_ps", [128, 1024], F32)
    dma("gpsimd", rotm_s[:, :], rotm_in.ap(), s_wl)
    q_wm = []
    evac = [0, 0]
    q_dma_v = [0, 0, 0, 0]
    k = 0
    for fq in range(NSL):
        slot = fq % 2
        if fq >= 2:
            wait("gpsimd", s_mm, q_wm[fq - 2])
        for c in range(DQA // 128):
            dma("gpsimd", qbw_sl[:, slot, c, :],
                qbw_t.ap()[c*128:(c+1)*128, fq*NFQ*128:(fq+1)*NFQ*128], s_wl)
        v_w = s_wl.n
        wait("tensor", s_wl, v_w)
        for f6 in range(NFQ):
            f = fq * NFQ + f6
            hslot = f % 4
            if f < H:
                wait("vector", s_bn, q_dma_v[hslot])
            for tb in range(TOK // 512):
                pp = q_ps[k % 2]
                wait("tensor", s_vx, evac[k % 2])
                for c in range(DQA // 128):
                    on("tensor", lambda e, slot=slot, c=c, f6=f6, tb=tb, pp=pp: e.matmul(
                        pp[:, :], qbw_sl[:, slot, c, f6*128:(f6+1)*128],
                        qan_fm[:, c, tb*512:(tb+1)*512],
                        start=(c == 0), stop=(c == DQA // 128 - 1)))
                mv = tbar()
                wait("vector", s_mm, mv)
                if f < H:
                    on("vector", lambda e, hslot=hslot, tb=tb, pp=pp: e.tensor_scalar(
                        qh_sb[:, hslot, tb*512:(tb+1)*512], pp[:, :], SCALE, None, OP.mult))
                else:
                    on("vector", lambda e, tb=tb, pp=pp: e.tensor_scalar(
                        qpe_bf[:, tb*512:(tb+1)*512], pp[:, :], SCALE, None, OP.mult))
                evac[k % 2] = vbar()
                k += 1
            if f >= H:
                p = f - H
                v_pe = s_vx.n
                wait("tensor", s_vx, v_pe)
                for tb in range(2):
                    on("tensor", lambda e, tb=tb: e.matmul(
                        rot_ps[:, tb*512:(tb+1)*512], rotm_s[:, :],
                        qpe_bf[:, tb*512:(tb+1)*512], start=True, stop=True))
                mv = tbar()
                wait("vector", s_mm, mv)
                wait("vector", s_bn, q_dma_v[(p + H) % 4])
                on("vector", lambda e: e.tensor_tensor(
                    rope_s1[:, :], qpe_bf[:, :], cosfm[:, :], OP.mult))
                on("vector", lambda e: e.tensor_tensor(
                    rope_s2[:, :], rot_ps[:, :], sinfm[:, :], OP.mult))
                wait("vector", s_vx, vbar())
                on("vector", lambda e, p=p: e.tensor_tensor(
                    qh_sb[:, (p + H) % 4, :], rope_s1[:, :], rope_s2[:, :], OP.add))
            v_f = vbar()
            wait("sync", s_vx, v_f)
            if f < H:
                q_dma_v[hslot] = dma(
                    "sync", fmb.ap()[R_QN + f*128: R_QN + (f+1)*128, :],
                    qh_sb[:, hslot, :], s_bn)
            else:
                p = f - H
                q_dma_v[(p + H) % 4] = dma(
                    "sync", fmb.ap()[R_QPE + p*128: R_QPE + (p+1)*128, :],
                    qh_sb[:, (p + H) % 4, :], s_bn)
        q_wm.append(s_mm.n)
    B.barrier()
    ph3b.close()
    early_tm.close()
    ph1.close()
    early_fm.close()

    wait("gpsimd", s_bn, s_bn.n)
    on("gpsimd", lambda e: e.collective_compute(
        "AllGather", OP.bypass, replica_groups=[list(range(NCORES))],
        ins=[fmb.ap().opt()], outs=[gfm.ap().opt()]), inc=s_cc)
    wait("gpsimd", s_cc, 1)
    B.barrier()

    if PH < 4:
        return finish()

    ph4 = ExitStack()
    qnh = B.sbuf(ph4, "qnh", [128, T], BF16)
    qph = B.sbuf(ph4, "qph", [64, T], BF16)
    knh = B.sbuf(ph4, "knh", [128, T], BF16)
    kph = B.sbuf(ph4, "kph", [64, T], BF16)
    vfmh = B.sbuf(ph4, "vfmh", [128, T], BF16)
    vh = B.sbuf(ph4, "vh", [128, T // 128, VD], BF16)
    p_sb = B.sbuf(ph4, "p_sb", [128, T], BF16)
    pt_sb = B.sbuf(ph4, "pt_sb", [128, 512], BF16)
    at_scr = B.sbuf(ph4, "at_scr", [128, 128], BF16)
    afm = B.sbuf(ph4, "afm", [128, T], BF16)
    negm = B.sbuf(ph4, "negm", [128, 16], F32)
    ssum = B.sbuf(ph4, "ssum", [128, 16], F32)
    alph = B.sbuf(ph4, "alph", [128, 16], F32)
    ttr_scr = B.sbuf(ph4, "ttr_scr", [128, 16], F32)
    nrow = B.sbuf(ph4, "nrow", [128, 1], F32)
    rcp = B.sbuf(ph4, "rcp", [128, 1], F32)

    sc_ps = [B.psum(ph4, "sc_ps%d" % i, [128, 512], F32) for i in range(2)]
    tr_ps4 = [B.psum(ph4, "tr4_%d" % i, [128, 1024], BF16) for i in range(2)]
    pv_ps = [B.psum(ph4, "pv_ps%d" % i, [128, 512], F32) for i in range(2)]

    wait("sync", s_cc, 1)
    for r in range(NCORES):
        dma("sync", kph[0:ROPE, r*TOK:(r+1)*TOK],
            gfm.ap()[FMROWS*r + R_KPE: FMROWS*r + R_KPE + ROPE, :], s_ld)
    v_kpe = s_ld.n

    def attn_loads(e, hd):
        with e.register("r_c%d" % hd) as r_c, e.register("r_h%d" % hd) as r_h, \
             e.register("r_o%d" % hd) as r_o:
            e.reg_load(r_c, cid_s[0:1, 0:1])
            e.reg_alu(r_h, r_c, 2, OP.mult)
            e.reg_alu(r_h, r_h, hd, OP.add)
            RB = FMROWS * TOK
            e.reg_alu(r_o, r_h, 128 * TOK, OP.mult)
            e.reg_alu(r_o, r_o, R_QN * TOK, OP.add)
            e.dma_start(qnh[:, :],
                        bass.AP(gfm, r_o, [[TOK, 128], [RB, NCORES], [1, TOK]])
                        ).then_inc(s_ld.h, 16)
            e.reg_alu(r_o, r_h, 64 * TOK, OP.mult)
            e.reg_alu(r_o, r_o, R_QPE * TOK, OP.add)
            e.dma_start(qph[0:64, :],
                        bass.AP(gfm, r_o, [[TOK, 64], [RB, NCORES], [1, TOK]])
                        ).then_inc(s_ld.h, 16)
            e.reg_alu(r_o, r_h, 128 * TOK, OP.mult)
            e.reg_alu(r_o, r_o, R_KN * TOK, OP.add)
            e.dma_start(knh[:, :],
                        bass.AP(gfm, r_o, [[TOK, 128], [RB, NCORES], [1, TOK]])
                        ).then_inc(s_ld.h, 16)
            e.reg_alu(r_o, r_h, 128 * TOK, OP.mult)
            e.reg_alu(r_o, r_o, R_VF * TOK, OP.add)
            e.dma_start(vfmh[:, :],
                        bass.AP(gfm, r_o, [[TOK, 128], [RB, NCORES], [1, TOK]])
                        ).then_inc(s_ld.h, 16)

    ATT_QT = int(os.environ.get("MK_ATT_QT", str(T // 128)))
    ATT_HD = int(os.environ.get("MK_ATT_HD", "2"))
    exp_v = [0, 0]
    tr_done = [0, 0]
    pv_evac = [0, 0]
    for hd in range(ATT_HD):
        s_ld.n += 4 * 16
        on("sync", lambda e, hd=hd: attn_loads(e, hd))
        v_qkv = s_ld.n
        wait("tensor", s_ld, v_qkv)
        if hd == 0:
            wait("tensor", s_ld, v_kpe)
        if hd == 0 and "att" in B.dbg:
            B.dbg_dump("att_gq", qnh[:, :], [128, T], BF16, s_out, waits=(s_ld,))
            B.dbg_dump("att_gk", knh[:, :], [128, T], BF16, s_out, waits=(s_ld,))
            B.dbg_dump("att_gqp", qph[0:64, :], [64, T], BF16, s_out, waits=(s_ld,))
            B.dbg_dump("att_gkp", kph[0:64, :], [64, T], BF16, s_out, waits=(s_ld,))
            B.dbg_dump("att_gvf", vfmh[:, :], [128, T], BF16, s_out, waits=(s_ld,))
        vtr = [0, 0]
        for s in range(T // 128):
            pp = tr_ps4[s % 2]
            wait("tensor", s_vx, vtr[s % 2])
            on("tensor", lambda e, s=s, pp=pp: e.transpose(
                pp[:, 0:128], vfmh[:, s*128:(s+1)*128], ident[:, :]))
            mv = tbar()
            wait("vector", s_mm, mv)
            on("vector", lambda e, s=s, pp=pp: e.tensor_copy(
                vh[:, s, :], pp[:, 0:128]))
            vtr[s % 2] = vbar()
        tr_done[0] = max(tr_done[0], vtr[0])
        tr_done[1] = max(tr_done[1], vtr[1])
        for qt in range(ATT_QT):
            KB = qt // 4 + 1
            for kb in range(KB):
                ns = 4 if kb < KB - 1 else (qt - 4 * (KB - 1) + 1)
                N = 128 * ns
                pp = sc_ps[kb % 2]
                wait("tensor", s_sc, exp_v[kb % 2])
                on("tensor", lambda e, qt=qt, kb=kb, N=N, pp=pp: e.matmul(
                    pp[:, 0:N], qnh[:, qt*128:(qt+1)*128],
                    knh[:, kb*512: kb*512+N], start=True, stop=False))
                on("tensor", lambda e, qt=qt, kb=kb, N=N, pp=pp: e.matmul(
                    pp[:, 0:N], qph[0:64, qt*128:(qt+1)*128],
                    kph[0:64, kb*512: kb*512+N], start=False, stop=True))
                mv = tbar()
                wait("vector", s_mm, mv)
                if kb == KB - 1:
                    on("vector", lambda e, N=N, pp=pp: e.tensor_tensor(
                        pp[:, N-128:N], pp[:, N-128:N], mask[:, :], OP.add))
                    wait("vector", s_vx, vbar())
                on("vector", lambda e, kb=kb, N=N, pp=pp: e.reduce_max(
                    negm[:, kb:kb+1], pp[:, 0:N], axis=AX, negate=True))
                vv = vbar()
                wait("scalar", s_vx, vv)
                exp_v[kb % 2] = on("scalar", lambda e, kb=kb, N=N, pp=pp: e.activation(
                    p_sb[:, kb*512: kb*512+N], pp[:, 0:N], AF.Exp,
                    bias=negm[:, kb:kb+1], accum_out=ssum[:, kb:kb+1]), inc=s_sc)
            v_exp_all = s_sc.n
            wait("vector", s_sc, v_exp_all)
            on("vector", lambda e, KB=KB: e.tensor_reduce(
                nrow[:, 0:1], negm[:, 0:KB], axis=AX, op=OP.min))
            vv = vbar()
            wait("vector", s_vx, vv)
            on("vector", lambda e, KB=KB: e.tensor_scalar(
                alph[:, 0:KB], negm[:, 0:KB], nrow[:, 0:1], None, OP.subtract))
            vv2 = vbar()
            wait("scalar", s_vx, vv2)
            sv = on("scalar", lambda e, KB=KB: e.activation(
                alph[:, 0:KB], alph[:, 0:KB], AF.Exp, scale=-1.0), inc=s_sc)
            wait("vector", s_sc, sv)
            on("vector", lambda e, KB=KB: e.tensor_tensor(
                ttr_scr[:, 0:KB], ssum[:, 0:KB], alph[:, 0:KB], OP.mult))
            wait("vector", s_vx, vbar())
            on("vector", lambda e, KB=KB: e.reduce_sum(
                nrow[:, 0:1], ttr_scr[:, 0:KB], axis=AX))
            vv3 = vbar()
            wait("vector", s_vx, vv3)
            on("vector", lambda e: e.reciprocal(rcp[:, 0:1], nrow[:, 0:1]))
            v_rcp = vbar()
            wait("tensor", s_vx, pv_evac[qt % 2])
            for kb in range(KB):
                ns = 4 if kb < KB - 1 else (qt - 4 * (KB - 1) + 1)
                N = 128 * ns
                on("vector", lambda e, kb=kb, N=N: e.tensor_scalar(
                    p_sb[:, kb*512: kb*512+N], p_sb[:, kb*512: kb*512+N],
                    alph[:, kb:kb+1], None, OP.mult))
                v_rs = vbar()
                pp = tr_ps4[kb % 2]
                wait("tensor", s_vx, tr_done[kb % 2])
                wait("tensor", s_vx, v_rs)
                for s in range(ns):
                    on("tensor", lambda e, kb=kb, s=s, pp=pp: e.transpose(
                        pp[:, s*128:(s+1)*128], p_sb[:, (kb*4+s)*128:(kb*4+s+1)*128],
                        ident[:, :]))
                mv = tbar()
                wait("vector", s_mm, mv)
                on("vector", lambda e, N=N, pp=pp: e.tensor_copy(
                    pt_sb[:, 0:N], pp[:, 0:N]))
                tr_done[kb % 2] = vbar()
                wait("tensor", s_vx, tr_done[kb % 2])
                for s in range(ns):
                    on("tensor", lambda e, qt=qt, kb=kb, s=s, KB=KB, ns=ns: e.matmul(
                        pv_ps[qt % 2][:, 0:VD], pt_sb[:, s*128:(s+1)*128],
                        vh[:, kb*4+s, :],
                        start=(kb == 0 and s == 0),
                        stop=(kb == KB - 1 and s == ns - 1),
                        skip_group_check=True))
            mv = tbar()
            wait("vector", s_mm, mv)
            wait("vector", s_vx, v_rcp)
            on("vector", lambda e, qt=qt: e.tensor_scalar(
                at_scr[:, :], pv_ps[qt % 2][:, 0:VD], rcp[:, 0:1], None, OP.mult))
            pv_evac[qt % 2] = vbar()
            ppt = tr_ps4[qt % 2]
            wait("tensor", s_vx, tr_done[qt % 2])
            wait("tensor", s_vx, pv_evac[qt % 2])
            on("tensor", lambda e, ppt=ppt: e.transpose(
                ppt[:, 0:128], at_scr[:, :], ident[:, :]))
            mv = tbar()
            wait("vector", s_mm, mv)
            on("vector", lambda e, qt=qt, ppt=ppt: e.tensor_copy(
                afm[:, qt*128:(qt+1)*128], ppt[:, 0:128]))
            tr_done[qt % 2] = vbar()
            if hd == 0 and qt in (0, 5) and "att" in B.dbg:
                pw = (qt // 4) * 512 + (qt % 4 + 1) * 128
                B.dbg_dump("att_p%d" % qt, p_sb[:, 0:pw], [128, pw], BF16, s_out, waits=(s_vx, s_sc))
                B.dbg_dump("att_negm%d" % qt, negm[:, 0:qt//4+1], [128, qt//4+1], F32, s_out, waits=(s_vx, s_sc))
                B.dbg_dump("att_ssum%d" % qt, ssum[:, 0:qt//4+1], [128, qt//4+1], F32, s_out, waits=(s_vx, s_sc))
                B.dbg_dump("att_alph%d" % qt, alph[:, 0:qt//4+1], [128, qt//4+1], F32, s_out, waits=(s_vx, s_sc))
                B.dbg_dump("att_nrow%d" % qt, nrow[:, :], [128, 1], F32, s_out, waits=(s_vx, s_sc))
                B.dbg_dump("att_rcp%d" % qt, rcp[:, :], [128, 1], F32, s_out, waits=(s_vx, s_sc))
                B.dbg_dump("att_scr%d" % qt, at_scr[:, :], [128, 128], BF16, s_out, waits=(s_vx, s_sc))
                B.dbg_dump("att_qnh%d" % qt, qnh[:, 0:1024], [128, 1024], BF16, s_out, waits=(s_vx, s_sc))
                B.dbg_dump("att_knh%d" % qt, knh[:, 0:1024], [128, 1024], BF16, s_out, waits=(s_vx, s_sc))
                B.dbg_dump("att_vh%d" % qt, vh[:, 0:8, :], [128, 8, VD], BF16, s_out, waits=(s_vx, s_sc))
        if hd == 0 and ATT_QT == T // 128 and "att" in B.dbg:
            B.dbg_dump("att_afm", afm[:, :], [128, T], BF16, s_out, waits=(s_vx, s_sc))
        if ATT_QT == T // 128:
            wait("sync", s_vx, s_vx.n)
            dma("sync", abn.ap()[hd*VD:(hd+1)*VD, :], afm[:, :], s_bn)
            wait("sync", s_bn, s_bn.n)

    if ATT_QT == T // 128 and ATT_HD == 2:
        wait("gpsimd", s_bn, s_bn.n)
        on("gpsimd", lambda e: e.collective_compute(
            "AllGather", OP.bypass, replica_groups=[list(range(NCORES))],
            ins=[abn.ap().opt()], outs=[gattn.ap().opt()]), inc=s_cc)
        wait("gpsimd", s_cc, s_cc.n)
    B.barrier()
    ph4.close()

    if PH < 5:
        return finish()

    ph5 = ExitStack()
    at_own = B.sbuf(ph5, "at_own", [128, H, TOK], BF16)
    ow_s = B.sbuf(ph5, "ow_s", [128, H, D], BF16)
    op_ps = [B.psum(ph5, "op_ps%d" % i, [128, 2048], F32) for i in range(2)]
    for c in range(H):
        dma("gpsimd", ow_s[:, c, :], ow_t.ap()[c*128:(c+1)*128, :], s_wl)
    v_wo = s_wl.n

    def attn_own_loads(e):
        with e.register("r_c") as r_c, e.register("r_o") as r_o:
            e.reg_load(r_c, cid_s[0:1, 0:1])
            for half in range(2):
                e.reg_alu(r_o, r_c, TOK, OP.mult)
                e.reg_alu(r_o, r_o, half * 8 * VD * T, OP.add)
                e.dma_start(at_own[:, half*8:(half+1)*8, :],
                            bass.AP(gattn, r_o, [[T, 128], [VD * T, 8], [1, TOK]])
                            ).then_inc(s_ld.h, 16)
    s_ld.n += 2 * 16
    on("sync", attn_own_loads)
    v_at = s_ld.n
    wait("tensor", s_ld, v_at)
    wait("tensor", s_wl, v_wo)
    res_v = [0, 0]
    for t in range(NT):
        pp = op_ps[t % 2]
        wait("tensor", s_vx, res_v[t % 2])
        for c in range(H):
            for nb in range(4):
                on("tensor", lambda e, t=t, c=c, nb=nb, pp=pp: e.matmul(
                    pp[:, nb*512:(nb+1)*512], at_own[:, c, t*128:(t+1)*128],
                    ow_s[:, c, nb*512:(nb+1)*512],
                    start=(c == 0), stop=(c == H - 1)))
        mv = tbar()
        wait("vector", s_mm, mv)
        on("vector", lambda e, t=t, pp=pp: e.tensor_tensor(
            x_tm[:, t, :], x_tm[:, t, :], pp[:, :], OP.add))
        res_v[t % 2] = vbar()
    B.barrier()
    ph5.close()

    if PH < 6:
        return finish()

    ph6 = ExitStack()
    xln2_fm = B.sbuf(ph6, "xln2_fm", [128, D // 128, TOK], BF16)
    cw_sb = B.sbuf(ph6, "cw_sb", [128, NT, NCH, 2], F32)
    cws_sb = B.sbuf(ph6, "cws_sb", [128, NT, NCH], F32)
    ph6b = ExitStack()
    ln2_scr = B.sbuf(ph6b, "ln2_scr", [128, NT, D], BF16)
    tr_ps6 = [B.psum(ph6b, "tr6_%d" % i, [128, 1024], BF16) for i in range(2)]
    v_stat2 = emit_norm_stats(lambda t: x_tm[:, t, :], NT, D)
    wait("vector", s_vx, v_stat2)
    for t in range(NT):
        on("vector", lambda e, t=t: e.tensor_scalar(
            ln2_scr[:, t, :], x_tm[:, t, :], st_m[:, t:t+1], st_r[:, t:t+1],
            OP.subtract, OP.mult))
    v_ln2 = vbar()
    pairs = [(ln2_scr[:, t, c*128:(c+1)*128], xln2_fm[:, c, t*128:(t+1)*128], c)
             for t in range(NT) for c in range(D // 128)]
    emit_transposes(pairs, tr_ps6, wait_sem=s_vx, wait_val=v_ln2)
    B.dbg_dump("xln2_fm", xln2_fm[:, :, :], [128, D // 128, TOK], BF16, s_out, waits=(s_vx, s_sc))
    B.barrier()
    ph6b.close()

    if PH < 7:
        return finish()

    ph7 = ExitStack()
    gate_s = B.sbuf(ph7, "gate_s", [128, H, NE], BF16)
    riota = B.sbuf(ph7, "riota", [128, NE], F32)
    lg_sb = B.sbuf(ph7, "lg_sb", [128, NT, NE], F32)
    exp_sb = B.sbuf(ph7, "exp_sb", [128, NT, NE], F32)
    sig_sb = B.sbuf(ph7, "sig_sb", [128, NT, NE], F32)
    sel_sb = B.sbuf(ph7, "sel_sb", [128, NT, NE], F32)
    r_t = B.sbuf(ph7, "r_t", [128, NE], F32)
    r_u = B.sbuf(ph7, "r_u", [128, NE], F32)
    r_m = B.sbuf(ph7, "r_m", [128, 1], F32)
    lg_ps = [B.psum(ph7, "lg_ps%d" % i, [128, 512], F32) for i in range(2)]
    for c in range(H):
        dma("gpsimd", gate_s[:, c, :], gate_t.ap()[c*128:(c+1)*128, :], s_wl)
    dma("sync", riota[:, :], riota_in.ap(), s_ld)
    v_wg = s_wl.n
    v_ri = s_ld.n
    wait("tensor", s_wl, v_wg)
    evac = [0, 0]
    for t in range(NT):
        pp = lg_ps[t % 2]
        wait("tensor", s_vx, evac[t % 2])
        for c in range(H):
            on("tensor", lambda e, t=t, c=c, pp=pp: e.matmul(
                pp[:, 0:NE], xln2_fm[:, c, t*128:(t+1)*128], gate_s[:, c, :],
                start=(c == 0), stop=(c == H - 1)))
        mv = tbar()
        wait("vector", s_mm, mv)
        on("vector", lambda e, t=t, pp=pp: e.tensor_copy(lg_sb[:, t, :], pp[:, 0:NE]))
        evac[t % 2] = vbar()
    v_lg = s_vx.n
    wait("vector", s_vx, v_lg)
    for t in range(NT):
        on("vector", lambda e, t=t: e.reduce_max(
            st_q[:, t:t+1], lg_sb[:, t, :], axis=AX, negate=True))
    vv = vbar()
    wait("scalar", s_vx, vv)
    for t in range(NT):
        on("scalar", lambda e, t=t: e.activation(
            exp_sb[:, t, :], lg_sb[:, t, :], AF.Exp, bias=st_q[:, t:t+1]))
    for t in range(NT):
        on("scalar", lambda e, t=t: e.activation(
            sig_sb[:, t, :], lg_sb[:, t, :], AF.Sigmoid), inc=s_sc)
    v_sg = s_sc.n
    wait("vector", s_sc, v_sg)
    wait("vector", s_ld, v_ri)
    on("vector", lambda e: e.memset(sel_sb[:, :, :], 0.0))
    for t in range(NT):
        on("vector", lambda e, t=t: e.tensor_copy(r_t[:, :], sig_sb[:, t, :]))
        wait("vector", s_vx, vbar())
        for kk in range(4):
            on("vector", lambda e: e.reduce_max(r_m[:, 0:1], r_t[:, :], axis=AX))
            wait("vector", s_vx, vbar())
            on("vector", lambda e: e.tensor_scalar(
                r_u[:, :], r_t[:, :], r_m[:, 0:1], None, OP.is_equal))
            wait("vector", s_vx, vbar())
            on("vector", lambda e: e.tensor_tensor(r_u[:, :], r_u[:, :], riota[:, :], OP.mult))
            wait("vector", s_vx, vbar())
            on("vector", lambda e: e.reduce_max(r_m[:, 0:1], r_u[:, :], axis=AX))
            wait("vector", s_vx, vbar())
            on("vector", lambda e: e.tensor_scalar(
                r_u[:, :], r_u[:, :], r_m[:, 0:1], None, OP.is_equal))
            wait("vector", s_vx, vbar())
            on("vector", lambda e, t=t: e.tensor_tensor(
                sel_sb[:, t, :], sel_sb[:, t, :], r_u[:, :], OP.add))
            on("vector", lambda e: e.scalar_tensor_tensor(
                r_t[:, :], r_u[:, :], -2.0, r_t[:, :], OP.mult, OP.add))
            wait("vector", s_vx, vbar())
    for t in range(NT):
        on("vector", lambda e, t=t: e.tensor_tensor(
            sel_sb[:, t, :], exp_sb[:, t, :], sel_sb[:, t, :], OP.mult))
        wait("vector", s_vx, vbar())
        on("vector", lambda e, t=t: e.reduce_sum(st_q[:, t:t+1], sel_sb[:, t, :], axis=AX))
    vv = vbar()
    wait("vector", s_vx, vv)
    on("vector", lambda e: e.reciprocal(st_r[:, 0:NT], st_q[:, 0:NT]))
    vv = vbar()
    wait("vector", s_vx, vv)
    for t in range(NT):
        on("vector", lambda e, t=t: e.tensor_scalar(
            cw_sb[:, t, :, :], sel_sb[:, t, :], st_r[:, t:t+1], None, OP.mult))
        wait("vector", s_vx, vbar())
        on("vector", lambda e, t=t: e.tensor_tensor(
            cws_sb[:, t, :], cw_sb[:, t, :, 0:1], cw_sb[:, t, :, 1:2], OP.add))
        wait("vector", s_vx, vbar())
        on("vector", lambda e, t=t: e.tensor_scalar(
            cws_sb[:, t, :], cws_sb[:, t, :], SMALL_SCALE, None, OP.mult))
    v_cw = vbar()
    B.dbg_dump("cw", cw_sb[:, :, :, :], [128, NT, NCH, 2], F32, s_out, waits=(s_vx, s_sc))
    B.dbg_dump("cws", cws_sb[:, :, :], [128, NT, NCH], F32, s_out, waits=(s_vx, s_sc))
    B.barrier()
    ph7.close()

    if PH < 8:
        return finish()

    ph8 = ExitStack()
    wgsl = B.sbuf(ph8, "wgsl", [128, 2, H, 384], BF16)
    wusl = B.sbuf(ph8, "wusl", [128, 2, H, 384], BF16)
    wdsl = B.sbuf(ph8, "wdsl", [128, DI // 128, D], BF16)
    a_sb = B.sbuf(ph8, "a_sb", [128, DI // 128, 2, 512], BF16)
    asil = B.sbuf(ph8, "asil", [128, 512], F32)
    gu_ps = [B.psum(ph8, "gu_ps%d" % i, [128, 512], F32) for i in range(4)]
    pd_ps = [B.psum(ph8, "pd_ps%d" % i, [128, 1024], F32) for i in range(2)]

    wait("vector", s_vx, v_cw)
    st = {"w_use": 0, "w_wm": [], "d_wm": [], "gu": 0,
          "gu_evac": [0, 0], "pd_evac": [0, 0], "a_consumed": 0,
          "sil_free": 0}

    def moe_expert(e_i, n_i, nhalf, g_src, u_src, d_src, cw_slice):
        if st["d_wm"]:
            wait("gpsimd", s_mm, st["d_wm"][-1])
        dma("gpsimd", wdsl[:, 0:n_i, :], d_src(e_i), s_wl)
        v_wd = s_wl.n
        n_per_half = n_i // nhalf
        for ihalf in range(nhalf):
            wslot = st["w_use"] % 2
            if st["w_use"] >= 2:
                wait("gpsimd", s_mm, st["w_wm"][st["w_use"] - 2])
            wd = 128 * n_per_half
            dma("gpsimd", wgsl[:, wslot, :, 0:wd], g_src(e_i, ihalf), s_wl)
            dma("gpsimd", wusl[:, wslot, :, 0:wd], u_src(e_i, ihalf), s_wl)
            st["w_use"] += 1
            v_w = s_wl.n
            wait("tensor", s_wl, v_w)
            for i3 in range(n_per_half):
                i = ihalf * n_per_half + i3
                for tb in range(2):
                    par = st["gu"] % 2
                    pg, pu = gu_ps[2 * par], gu_ps[2 * par + 1]
                    wait("tensor", s_vx, st["gu_evac"][par])
                    for hh in range(H):
                        on("tensor", lambda e, wslot=wslot, i3=i3, hh=hh, tb=tb, pg=pg: e.matmul(
                            pg[:, :], wgsl[:, wslot, hh, i3*128:(i3+1)*128],
                            xln2_fm[:, hh, tb*512:(tb+1)*512],
                            start=(hh == 0), stop=(hh == H - 1)))
                    for hh in range(H):
                        on("tensor", lambda e, wslot=wslot, i3=i3, hh=hh, tb=tb, pu=pu: e.matmul(
                            pu[:, :], wusl[:, wslot, hh, i3*128:(i3+1)*128],
                            xln2_fm[:, hh, tb*512:(tb+1)*512],
                            start=(hh == 0), stop=(hh == H - 1)))
                    mv = tbar()
                    wait("scalar", s_mm, mv)
                    wait("scalar", s_vx, st["sil_free"])
                    sv = on("scalar", lambda e, pg=pg: e.activation(
                        asil[:, :], pg[:, :], AF.Silu), inc=s_sc)
                    wait("vector", s_sc, sv)
                    wait("vector", s_mm, st["a_consumed"])
                    on("vector", lambda e, i=i, tb=tb, pu=pu: e.tensor_tensor(
                        a_sb[:, i, tb, :], asil[:, :], pu[:, :], OP.mult))
                    st["gu_evac"][par] = vbar()
                    st["sil_free"] = st["gu_evac"][par]
                    st["gu"] += 1
            st["w_wm"].append(s_mm.n)
        v_a = s_vx.n
        wait("tensor", s_vx, v_a)
        wait("tensor", s_wl, v_wd)
        for tb in range(2):
            for t4 in range(4):
                t_glob = tb * 4 + t4
                for half in range(2):
                    php = pd_ps[half]
                    wait("tensor", s_vx, st["pd_evac"][half])
                    for i in range(n_i):
                        for nb in range(2):
                            on("tensor", lambda e, i=i, t4=t4, tb=tb, half=half, nb=nb, php=php: e.matmul(
                                php[:, nb*512:(nb+1)*512],
                                a_sb[:, i, tb, t4*128:(t4+1)*128],
                                wdsl[:, i, half*1024 + nb*512: half*1024 + (nb+1)*512],
                                start=(i == 0), stop=(i == n_i - 1)))
                    mv = tbar()
                    wait("vector", s_mm, mv)
                    on("vector", lambda e, t_glob=t_glob, half=half, php=php,
                            cw=cw_slice(e_i, t_glob): e.scalar_tensor_tensor(
                        x_tm[:, t_glob, half*1024:(half+1)*1024], php[:, :], cw,
                        x_tm[:, t_glob, half*1024:(half+1)*1024], OP.mult, OP.add))
                    st["pd_evac"][half] = vbar()
        st["a_consumed"] = s_mm.n
        st["d_wm"].append(s_mm.n)

    rr = lambda w: w.ap().rearrange("e (h p) c -> e p h c", p=128)
    g_src = lambda e_i, ihalf: rr(wg_w)[e_i, :, :, ihalf*384:(ihalf+1)*384]
    u_src = lambda e_i, ihalf: rr(wu_w)[e_i, :, :, ihalf*384:(ihalf+1)*384]
    d_src = lambda e_i: wd_w.ap().rearrange("e (i p) d -> e p i d", p=128)[e_i, :, :, :]
    cw_slice = lambda e_i, t: cw_sb[:, t, e_i // 2, e_i % 2: e_i % 2 + 1]
    for e_i in range(NE):
        moe_expert(e_i, DI // 128, 2, g_src, u_src, d_src, cw_slice)

    cg_src = lambda e_i, ihalf: rr(cwg_w)[e_i, :, :, :]
    cu_src = lambda e_i, ihalf: rr(cwu_w)[e_i, :, :, :]
    cd_src = lambda e_i: cwd_w.ap().rearrange("e (i p) d -> e p i d", p=128)[e_i, :, :, :]
    cws_slice = lambda e_i, t: cws_sb[:, t, e_i:e_i+1]
    for e_i in range(NCH):
        moe_expert(e_i, 1, 1, cg_src, cu_src, cd_src, cws_slice)
    B.barrier()
    ph8.close()

    return finish()



def _host_prep(inputs):
    f32 = np.float32
    x = np.asarray(inputs["L0_attn_hidden_states"], f32).reshape(T, D)
    q_a_w = np.asarray(inputs["L0_attn_q_a_w"], f32)
    q_a_ln = np.asarray(inputs["L0_attn_q_a_ln_w"], f32)
    q_b_w = np.asarray(inputs["L0_attn_q_b_w"], f32)
    kv_a_w = np.asarray(inputs["L0_attn_kv_a_w"], f32)
    kv_a_ln = np.asarray(inputs["L0_attn_kv_a_ln_w"], f32)
    kv_b_w = np.asarray(inputs["L0_attn_kv_b_w"], f32)
    o_w = np.asarray(inputs["L0_attn_o_w"], f32)
    gate_w = np.asarray(inputs["L0_ffn_gate_w"], f32)

    qaw_t = np.ascontiguousarray(q_a_w.T)
    qb = q_b_w.reshape(H, QHD, DQA)
    qb_nope = qb[:, :NOPE, :].reshape(H * NOPE, DQA)
    qb_pe = qb[:, NOPE:, :].reshape(H * ROPE, DQA)
    qbw_t = np.ascontiguousarray(np.concatenate([qb_nope, qb_pe], 0).T)
    qa_gamma = np.ascontiguousarray(q_a_ln.reshape(DQA // 128, 128).T)
    kvaw_t = np.ascontiguousarray(kv_a_w.T)
    kvb = kv_b_w.reshape(H, NOPE + VD, DCKV)
    kvb_n = kvb[:, :NOPE, :].reshape(H * NOPE, DCKV)
    kvb_v = kvb[:, NOPE:, :].reshape(H * VD, DCKV)
    kvbw_t = np.ascontiguousarray(np.concatenate([kvb_n, kvb_v], 0).T)
    kva_gamma = np.ascontiguousarray(kv_a_ln.reshape(DCKV // 128, 128).T)
    ow_t = np.ascontiguousarray(o_w.T)
    gate_t = np.ascontiguousarray(gate_w.T)

    ident = np.eye(128, dtype=ml_dtypes.bfloat16)
    masktile = np.triu(np.full((128, 128), NEGBIG, f32), k=1)
    inv_freq = 1.0 / (10000.0 ** (np.arange(0, ROPE, 2, dtype=f32) / ROPE))
    tpos = np.arange(T, dtype=f32)
    freqs = np.outer(tpos, inv_freq)
    emb = np.concatenate([freqs, freqs], -1)
    cos = np.cos(emb).astype(f32)
    sin = np.sin(emb).astype(f32)
    sin_eff = sin.copy()
    sin_eff[:, :ROPE // 2] *= -1.0
    rotm = np.zeros((128, 128), np.float32)
    for blk in range(2):
        b = blk * ROPE
        for r in range(ROPE // 2):
            rotm[b + r + ROPE // 2, b + r] = -1.0
            rotm[b + r, b + r + ROPE // 2] = 1.0
    rotm = rotm.astype(ml_dtypes.bfloat16)
    riota = np.ascontiguousarray(
        np.broadcast_to(np.arange(NE, 0, -1, dtype=f32), (128, NE)))

    shared = dict(
        qaw_t=qaw_t, qbw_t=qbw_t, qa_gamma=qa_gamma, kvaw_t=kvaw_t,
        kvbw_t=kvbw_t, kva_gamma=kva_gamma, ow_t=ow_t, gate_t=gate_t,
        Wg=np.ascontiguousarray(np.asarray(inputs["L0_ffn_Wg"], f32)),
        Wu=np.ascontiguousarray(np.asarray(inputs["L0_ffn_Wu"], f32)),
        Wd=np.ascontiguousarray(np.asarray(inputs["L0_ffn_Wd"], f32)),
        cWg=np.ascontiguousarray(np.asarray(inputs["L0_ffn_cWg"], f32)),
        cWu=np.ascontiguousarray(np.asarray(inputs["L0_ffn_cWu"], f32)),
        cWd=np.ascontiguousarray(np.asarray(inputs["L0_ffn_cWd"], f32)),
        ident_c=ident, masktile=masktile, riota_c=riota,
        epsc=np.ascontiguousarray(np.broadcast_to(np.array([EPS_LN, EPS_RMS], f32), (128, 2))),
        rotm=rotm,
    )
    per_core = []
    for j in range(NCORES):
        sl = slice(j * TOK, (j + 1) * TOK)
        cos_sl = cos[sl].T
        sin_sl = sin_eff[sl].T
        m = dict(
            x_own=np.ascontiguousarray(x[sl]),
            cosfm_c=np.ascontiguousarray(np.concatenate([cos_sl, cos_sl], 0)),
            sinfm_c=np.ascontiguousarray(np.concatenate([sin[sl].T, sin[sl].T], 0)),
            costm_c=np.ascontiguousarray(cos[sl]),
            sintm_c=np.ascontiguousarray(sin_eff[sl]),
            cid=np.array([[j]], np.int32),
        )
        m.update(shared)
        per_core.append(m)
    return per_core


def kernel(**inputs):
    nc, dbg_outs = build_program()
    maps = _host_prep(inputs)
    res = run_bass_kernel_spmd(nc, maps, core_ids=list(range(NCORES)))
    outs = [np.asarray(r["y"], np.float32) for r in res.results]
    y = np.concatenate(outs, 0).reshape(1, T, D)
    if dbg_outs:
        kernel.debug = [{k: r["dbg_" + k] for k in dbg_outs} for r in res.results]
    return y


# version 09
# speedup vs baseline: 1.2540x; 1.2540x over previous
import os
import numpy as np
import ml_dtypes
from contextlib import ExitStack

import concourse.bass as bass
import concourse.mybir as mybir
from concourse.bass_utils import run_bass_kernel_spmd

F32 = mybir.dt.float32
BF16 = mybir.dt.bfloat16
I32 = mybir.dt.int32
AX = mybir.AxisListType.X
OP = mybir.AluOpType
AF = mybir.ActivationFunctionType

NCORES = 8
T = 8192
D = 2048
TOK = T // NCORES
NT = TOK // 128
H = 16
NOPE, ROPE, VD = 128, 64, 128
QHD = NOPE + ROPE
SCALE = QHD ** -0.5
DQA = 1536
DCKV = 512
DKVA = DCKV + ROPE
NE, DI = 32, 768
NCH, DIC = 16, 128
SMALL_SCALE = 0.05
EPS_LN = 1e-5
EPS_RMS = 1e-6
NEGBIG = -1e30

R_KN = 0
R_KPE = 2048
R_VF = 2112
KVROWS = 4160
R_QN = 0
R_QPE = 2048
QROWS = 3072

ENGINES = ("tensor", "vector", "scalar", "gpsimd", "sync")


class Sem:
    __slots__ = ("h", "n")

    def __init__(self, stack, nc, name):
        self.h = stack.enter_context(nc.semaphore(name))
        self.n = 0


class Build:
    def __init__(self):
        self.nc = bass.Bass("TRN2", target_bir_lowering=False, detect_race_conditions=False)
        self.root = ExitStack()
        self.q = {e: [] for e in ENGINES}
        self.dbg = set(x for x in os.environ.get("MK_DEBUG_OUT", "").split(",") if x)
        if "att" in self.dbg:
            self.dbg |= {"att_gq", "att_gk", "att_gqp", "att_gkp", "att_gvf", "att_afm"}
        if "gbuf" in self.dbg:
            self.dbg |= {"g_kv", "g_q", "g_a0", "b_kv", "b_q"}
            self.dbg |= {"att_%s%d" % (n, q) for q in (0, 5) for n in
                         ("p", "negm", "ssum", "alph", "nrow", "rcp", "scr", "qnh", "knh", "vh")}
        self.dbg_outs = {}

    def on(self, eng, fn, inc=None, k=1):
        if inc is not None:
            inc.n += k
            def wrapped(e, fn=fn, inc=inc, k=k):
                r = fn(e)
                r.then_inc(inc.h, k)
                return r
            self.q[eng].append(wrapped)
            return inc.n
        self.q[eng].append(fn)
        return None

    def last_inc(self, eng, sem, k=1):
        sem.n += k
        idx = len(self.q[eng]) - 1
        while getattr(self.q[eng][idx], "is_wait", False):
            idx -= 1
        prev = self.q[eng][idx]
        def wrapped(e, prev=prev, sem=sem, k=k):
            r = prev(e)
            r.then_inc(sem.h, k)
            return r
        self.q[eng][idx] = wrapped
        return sem.n

    def wait(self, eng, sem, val):
        if val > 0:
            f = lambda e, s=sem, v=val: e.wait_ge(s.h, v)
            f.is_wait = True
            self.q[eng].append(f)

    def dma(self, eng, out_ap, in_ap, sem):
        return self.on(eng, lambda e, o=out_ap, i=in_ap: e.dma_start(o, i),
                       inc=sem, k=16)

    def barrier(self):
        nc = self.nc
        with nc.Block() as blk:
            for name in ENGINES:
                ops = self.q[name]
                if not ops:
                    continue
                def runner(e, ops=ops):
                    for f in ops:
                        f(e)
                getattr(blk, name)(runner)
        self.q = {e: [] for e in ENGINES}

    def sbuf(self, stack, name, shape, dt, side="left"):
        return stack.enter_context(self.nc.sbuf_tensor(name, shape, dt, side=side))

    def psum(self, stack, name, shape, dt):
        return stack.enter_context(self.nc.psum_tensor(name, shape, dt))

    def dbg_dump(self, name, src_ap, shape, dt, sem, waits=()):
        if name not in self.dbg:
            return
        h = self.nc.dram_tensor("dbg_" + name, shape, dt, kind="ExternalOutput")
        self.dbg_outs[name] = h
        for wsem in waits:
            self.wait("sync", wsem, wsem.n)
        self.dma("sync", h.ap(), src_ap, sem)


def build_program():
    B = Build()
    nc = B.nc
    root = B.root
    PH = int(os.environ.get("MK_PHASES", "99"))

    dram_in = lambda n, s, dt=F32: nc.dram_tensor(n, s, dt, kind="ExternalInput")
    dram_out = lambda n, s, dt=F32: nc.dram_tensor(n, s, dt, kind="ExternalOutput")

    x_in = dram_in("x_own", [TOK, D])
    qaw_t = dram_in("qaw_t", [D, DQA])
    qbw_t = dram_in("qbw_t", [DQA, H * QHD])
    qa_g = dram_in("qa_gamma", [128, DQA // 128])
    kvaw_t = dram_in("kvaw_t", [D, DKVA])
    kvbw_t = dram_in("kvbw_t", [DCKV, H * (NOPE + VD)])
    kva_g = dram_in("kva_gamma", [128, DCKV // 128])
    ow_t = dram_in("ow_t", [H * VD, D])
    gate_t = dram_in("gate_t", [D, NE])
    wg_w = dram_in("Wg", [NE, D, DI])
    wu_w = dram_in("Wu", [NE, D, DI])
    wd_w = dram_in("Wd", [NE, DI, D])
    cwg_w = dram_in("cWg", [NCH, D, DIC])
    cwu_w = dram_in("cWu", [NCH, D, DIC])
    cwd_w = dram_in("cWd", [NCH, DIC, D])
    ident_in = dram_in("ident_c", [128, 128], BF16)
    mask_in = dram_in("masktile", [128, 128])
    cosfm_in = dram_in("cosfm_c", [128, TOK])
    sinfm_in = dram_in("sinfm_c", [128, TOK])
    costm_in = dram_in("costm_c", [TOK, ROPE])
    sintm_in = dram_in("sintm_c", [TOK, ROPE])
    riota_in = dram_in("riota_c", [128, NE])
    epsc_in = dram_in("epsc", [128, 2])
    rotm_in = dram_in("rotm", [128, 128], BF16)
    cid_in = dram_in("cid", [1, 1], I32)

    y_out = dram_out("y", [TOK, D])

    kvb = nc.dram_tensor("kvb", [KVROWS, TOK], BF16)
    qbb = nc.dram_tensor("qbb", [QROWS, TOK], BF16)
    gkv = nc.dram_tensor("gkv", [NCORES * KVROWS, TOK], BF16, addr_space="Shared")
    gq = nc.dram_tensor("gq", [NCORES * QROWS, TOK], BF16, addr_space="Shared")
    abn0 = nc.dram_tensor("abn0", [VD, T], BF16)
    abn1 = nc.dram_tensor("abn1", [VD, T], BF16)
    ga0 = nc.dram_tensor("ga0", [NCORES * VD, T], BF16, addr_space="Shared")
    ga1 = nc.dram_tensor("ga1", [NCORES * VD, T], BF16, addr_space="Shared")

    x_tm = B.sbuf(root, "x_tm", [128, NT, D], F32)
    ident = B.sbuf(root, "ident", [128, 128], BF16)
    mask = B.sbuf(root, "mask", [128, 128], F32)
    sq_scr = B.sbuf(root, "sq_scr", [128, D], BF16)
    st_s = B.sbuf(root, "st_s", [128, NT], F32)
    st_q = B.sbuf(root, "st_q", [128, NT], F32)
    st_m = B.sbuf(root, "st_m", [128, NT], F32)
    st_r = B.sbuf(root, "st_r", [128, NT], F32)
    cid_s = B.sbuf(root, "cid_s", [1, 1], I32)
    eps_s = B.sbuf(root, "eps_s", [128, 2], F32)

    s_ld = Sem(root, nc, "s_ld")
    s_mm = Sem(root, nc, "s_mm")
    s_vx = Sem(root, nc, "s_vx")
    s_sc = Sem(root, nc, "s_sc")
    s_bn = Sem(root, nc, "s_bn")
    s_wl = Sem(root, nc, "s_wl")
    s_fr = Sem(root, nc, "s_fr")
    s_fd = Sem(root, nc, "s_fd")
    s_cc = Sem(root, nc, "s_cc")
    s_out = Sem(root, nc, "s_out")

    on, wait, dma = B.on, B.wait, B.dma

    def vbar():
        return on("vector", lambda e: e.engine_nop(), inc=s_vx)

    def tbar():
        return B.last_inc("tensor", s_mm)

    early_fm = ExitStack()
    ph1 = ExitStack()
    early_tm = ExitStack()

    cosfm = B.sbuf(early_fm, "cosfm", [128, TOK], F32)
    sinfm = B.sbuf(early_fm, "sinfm", [128, TOK], F32)
    qa_gs = B.sbuf(early_fm, "qa_gs", [128, DQA // 128], F32)
    kva_gs = B.sbuf(early_fm, "kva_gs", [128, DCKV // 128], F32)
    xln1_fm = B.sbuf(ph1, "xln1_fm", [128, D // 128, TOK], BF16)
    costm = B.sbuf(early_tm, "costm", [128, NT, ROPE], F32, side="right")
    sintm = B.sbuf(early_tm, "sintm", [128, NT, ROPE], F32, side="right")

    dma("sync", ident[:, :], ident_in.ap(), s_ld)
    dma("sync", mask[:, :], mask_in.ap(), s_ld)
    dma("sync", cosfm[:, :], cosfm_in.ap(), s_ld)
    dma("sync", sinfm[:, :], sinfm_in.ap(), s_ld)
    dma("sync", costm[:, :, :], costm_in.ap().rearrange("(n p) r -> p n r", p=128), s_ld)
    dma("sync", sintm[:, :, :], sintm_in.ap().rearrange("(n p) r -> p n r", p=128), s_ld)
    dma("sync", qa_gs[:, :], qa_g.ap(), s_ld)
    dma("sync", kva_gs[:, :], kva_g.ap(), s_ld)
    dma("sync", cid_s[:, :], cid_in.ap(), s_ld)
    dma("sync", eps_s[:, :], epsc_in.ap(), s_ld)
    dma("sync", x_tm[:, :, :], x_in.ap().rearrange("(n p) d -> p n d", p=128), s_ld)
    v_x = s_ld.n

    def emit_norm_stats(src_fn, n_tiles, width):
        for t in range(n_tiles):
            src = src_fn(t)
            if t > 0:
                wait("scalar", s_sc, s_sc.n)
            on("scalar", lambda e, s=src, t=t: e.activation(
                sq_scr[:, 0:width], s, AF.Square, accum_out=st_q[:, t:t+1]), inc=s_sc)
            on("vector", lambda e, s=src, t=t: e.reduce_sum(
                st_s[:, t:t+1], s, axis=AX))
        v_s = s_sc.n
        vv = vbar()
        wait("vector", s_vx, vv)
        wait("vector", s_sc, v_s)
        on("vector", lambda e: e.tensor_scalar(
            st_m[:, 0:n_tiles], st_s[:, 0:n_tiles], 1.0 / width, None, OP.mult))
        wait("vector", s_vx, vbar())
        on("vector", lambda e: e.tensor_tensor(
            st_s[:, 0:n_tiles], st_m[:, 0:n_tiles], st_m[:, 0:n_tiles], OP.mult))
        wait("vector", s_vx, vbar())
        on("vector", lambda e: e.scalar_tensor_tensor(
            st_q[:, 0:n_tiles], st_q[:, 0:n_tiles], 1.0 / width,
            st_s[:, 0:n_tiles], OP.mult, OP.subtract))
        vv2 = vbar()
        wait("scalar", s_vx, vv2)
        v_s2 = on("scalar", lambda e: e.activation(
            st_q[:, 0:n_tiles], st_q[:, 0:n_tiles], AF.Sqrt, bias=eps_s[:, 0:1]), inc=s_sc)
        wait("vector", s_sc, v_s2)
        on("vector", lambda e: e.reciprocal(st_r[:, 0:n_tiles], st_q[:, 0:n_tiles]))
        return vbar()

    def emit_rms_tile(t, width, src_psum):
        wait("scalar", s_sc, s_sc.n)
        on("scalar", lambda e, t=t, s=src_psum: e.activation(
            sq_scr[:, 0:width], s, AF.Square, accum_out=st_q[:, t:t+1]), inc=s_sc)
        v0 = s_sc.n
        wait("scalar", s_sc, v0)
        sv = on("scalar", lambda e, t=t: e.activation(
            st_q[:, t:t+1], st_q[:, t:t+1], AF.Sqrt, bias=eps_s[:, 1:2], scale=1.0 / width), inc=s_sc)
        wait("vector", s_sc, sv)
        on("vector", lambda e, t=t: e.reciprocal(st_r[:, t:t+1], st_q[:, t:t+1]))
        vv = vbar()
        wait("vector", s_vx, vv)

    def emit_transposes(pairs, tr_ps, gamma=None, scale=None, wait_sem=None,
                        wait_val=None):
        if wait_val:
            wait("tensor", wait_sem, wait_val)
        evac = [0, 0]
        for k, (src, dst, c) in enumerate(pairs):
            pp = tr_ps[k % 2]
            wait("tensor", s_vx, evac[k % 2])
            on("tensor", lambda e, s=src, pp=pp: e.transpose(pp[:, 0:128], s, ident[:, :]))
            mv = tbar()
            wait("vector", s_mm, mv)
            if gamma is not None:
                on("vector", lambda e, d=dst, pp=pp, c=c: e.tensor_scalar(
                    d, pp[:, 0:128], gamma[:, c:c+1], None, OP.mult))
            elif scale is not None:
                on("vector", lambda e, d=dst, pp=pp: e.tensor_scalar(
                    d, pp[:, 0:128], float(scale), None, OP.mult))
            else:
                on("vector", lambda e, d=dst, pp=pp: e.tensor_copy(d, pp[:, 0:128]))
            evac[k % 2] = vbar()
        return s_vx.n

    def finish():
        dma("sync", y_out.ap().rearrange("(n p) d -> p n d", p=128), x_tm[:, :, :], s_out)
        wait("sync", s_out, s_out.n)
        B.barrier()
        return nc, B.dbg_outs

    ph1b = ExitStack()
    ln_scr = B.sbuf(ph1b, "ln_scr", [128, NT, D], BF16)
    tr_ps1 = [B.psum(ph1b, "tr1_%d" % i, [128, 1024], BF16) for i in range(2)]

    wait("scalar", s_ld, v_x)
    wait("vector", s_ld, v_x)
    wait("tensor", s_ld, v_x)
    v_stat = emit_norm_stats(lambda t: x_tm[:, t, :], NT, D)
    wait("vector", s_vx, v_stat)
    for t in range(NT):
        on("vector", lambda e, t=t: e.tensor_scalar(
            ln_scr[:, t, :], x_tm[:, t, :], st_m[:, t:t+1], st_r[:, t:t+1],
            OP.subtract, OP.mult))
    v_ln1 = vbar()
    pairs = [(ln_scr[:, t, c*128:(c+1)*128], xln1_fm[:, c, t*128:(t+1)*128], c)
             for t in range(NT) for c in range(D // 128)]
    B.dbg_dump("ln_tm", ln_scr[:, :, :], [128, NT, D], BF16, s_out, waits=(s_vx, s_sc))
    B.dbg_dump("stats_m", st_m[:, :], [128, NT], F32, s_out, waits=(s_vx, s_sc))
    B.dbg_dump("stats_r", st_r[:, :], [128, NT], F32, s_out, waits=(s_vx, s_sc))
    emit_transposes(pairs, tr_ps1, wait_sem=s_vx, wait_val=v_ln1)
    B.dbg_dump("xln1_fm", xln1_fm[:, :, :], [128, D // 128, TOK], BF16, s_out, waits=(s_vx, s_sc))
    B.barrier()
    ph1b.close()

    if PH < 2:
        return finish()

    ph2k = ExitStack()
    ckvn_fm = B.sbuf(ph2k, "ckvn_fm", [128, DCKV // 128, TOK], BF16)
    kpe_fm = B.sbuf(ph2k, "kpe_fm", [64, TOK], BF16)

    ph2a = ExitStack()
    kvaw_s = B.sbuf(ph2a, "kvaw_s", [128, D // 128, DKVA], BF16)
    for c in range(D // 128):
        dma("gpsimd", kvaw_s[:, c, :], kvaw_t.ap()[c*128:(c+1)*128, :], s_wl)
    v_wkva = s_wl.n
    ckv_ps = [B.psum(ph2a, "ckv_ps%d" % i, [128, 1024], F32) for i in range(2)]
    tr_ps2 = [B.psum(ph2a, "tr2_%d" % i, [128, 1024], BF16) for i in range(2)]
    ckvn_tm = B.sbuf(ph2a, "ckvn_tm", [128, NT, DCKV], BF16)
    kpe_tm = B.sbuf(ph2a, "kpe_tm", [128, NT, ROPE], F32)
    kpe_rot = B.sbuf(ph2a, "kpe_rot", [128, NT, ROPE], BF16)
    rop_a = B.sbuf(ph2a, "rop_a", [128, ROPE], F32)
    rop_b = B.sbuf(ph2a, "rop_b", [128, ROPE], F32)

    wait("tensor", s_wl, v_wkva)
    evac_v = [0, 0]
    rope_war = [0]
    for t in range(NT):
        pp = ckv_ps[t % 2]
        wait("tensor", s_vx, evac_v[t % 2])
        for (lo, hi) in ((0, 512), (512, DKVA)):
            for c in range(D // 128):
                on("tensor", lambda e, t=t, c=c, lo=lo, hi=hi, pp=pp: e.matmul(
                    pp[:, lo:hi], xln1_fm[:, c, t*128:(t+1)*128],
                    kvaw_s[:, c, lo:hi],
                    start=(c == 0), stop=(c == D // 128 - 1)))
        mv = tbar()
        wait("scalar", s_mm, mv)
        emit_rms_tile(t, DCKV, pp[:, 0:DCKV])
        on("vector", lambda e, t=t, pp=pp: e.tensor_scalar(
            ckvn_tm[:, t, :], pp[:, 0:DCKV], st_r[:, t:t+1], None, OP.mult))
        on("vector", lambda e, t=t, pp=pp: e.tensor_copy(
            kpe_tm[:, t, :], pp[:, DCKV:DKVA]))
        HR = ROPE // 2
        wait("vector", s_vx, rope_war[0])
        def rope_tm1(e, t=t):
            kt = kpe_tm[:, t, :]
            e.tensor_tensor(rop_a[:, 0:HR], kt[:, HR:ROPE], sintm[:, t, 0:HR], OP.mult)
            e.tensor_tensor(rop_a[:, HR:ROPE], kt[:, 0:HR], sintm[:, t, HR:ROPE], OP.mult)
            e.tensor_tensor(rop_b[:, 0:ROPE], kt[:, 0:ROPE], costm[:, t, 0:ROPE], OP.mult)
        on("vector", rope_tm1)
        wait("vector", s_vx, vbar())
        on("vector", lambda e, t=t: e.tensor_tensor(
            kpe_rot[:, t, :], rop_b[:, 0:ROPE], rop_a[:, 0:ROPE], OP.add))
        evac_v[t % 2] = vbar()
        rope_war[0] = evac_v[t % 2]
    v_ckv_done = s_vx.n

    pairs = [(ckvn_tm[:, t, c*128:(c+1)*128], ckvn_fm[:, c, t*128:(t+1)*128], c)
             for t in range(NT) for c in range(DCKV // 128)]
    emit_transposes(pairs, tr_ps2, gamma=kva_gs, wait_sem=s_vx, wait_val=v_ckv_done)
    evac = [s_vx.n, s_vx.n]
    for t in range(NT):
        pp = tr_ps2[t % 2]
        wait("tensor", s_vx, evac[t % 2])
        on("tensor", lambda e, t=t, pp=pp: e.transpose(
            pp[0:ROPE, 0:128], kpe_rot[:, t, :], ident[:, :]))
        mv = tbar()
        wait("vector", s_mm, mv)
        on("vector", lambda e, t=t, pp=pp: e.tensor_copy(
            kpe_fm[0:ROPE, t*128:(t+1)*128], pp[0:ROPE, 0:128]))
        evac[t % 2] = vbar()
    B.dbg_dump("ckvn_fm", ckvn_fm[:, :, :], [128, DCKV // 128, TOK], BF16, s_out, waits=(s_vx, s_sc))
    B.dbg_dump("kpe_fm", kpe_fm[0:64, :], [64, TOK], BF16, s_out, waits=(s_vx, s_sc))
    B.barrier()
    ph2a.close()
    if int(os.environ.get("MK_P2", "9")) < 3:
        return finish()

    ph2b = ExitStack()
    kvbw_s = B.sbuf(ph2b, "kvbw_s", [128, DCKV // 128, H * (NOPE + VD)], BF16)
    for c in range(DCKV // 128):
        dma("gpsimd", kvbw_s[:, c, :], kvbw_t.ap()[c*128:(c+1)*128, :], s_wl)
    v_wkvb = s_wl.n
    kn_hd = B.sbuf(ph2b, "kn_hd", [128, 2, TOK], BF16)
    kn_ps = [B.psum(ph2b, "kn_ps%d" % i, [128, 512], F32) for i in range(2)]
    wait("tensor", s_wl, v_wkvb)
    kn_dma_v = [0, 0]
    evac = [0, 0]
    k = 0
    for f in range(2 * H):
        slot = f % 2
        wait("vector", s_bn, kn_dma_v[slot])
        for tb in range(TOK // 512):
            pp = kn_ps[k % 2]
            wait("tensor", s_vx, evac[k % 2])
            for c in range(DCKV // 128):
                on("tensor", lambda e, f=f, tb=tb, c=c, pp=pp: e.matmul(
                    pp[:, :], kvbw_s[:, c, f*128:(f+1)*128],
                    ckvn_fm[:, c, tb*512:(tb+1)*512],
                    start=(c == 0), stop=(c == DCKV // 128 - 1)))
            mv = tbar()
            wait("vector", s_mm, mv)
            on("vector", lambda e, tb=tb, pp=pp, slot=slot: e.tensor_copy(
                kn_hd[:, slot, tb*512:(tb+1)*512], pp[:, :]))
            evac[k % 2] = vbar()
            k += 1
        wait("sync", s_vx, s_vx.n)
        row = R_KN + f * 128 if f < H else R_VF + (f - H) * 128
        kn_dma_v[slot] = dma(
            "sync", kvb.ap()[row: row + 128, :], kn_hd[:, slot, :], s_bn)
    wait("sync", s_vx, s_vx.n)
    dma("sync", kvb.ap()[R_KPE:R_KPE + ROPE, :], kpe_fm[0:ROPE, :], s_bn)
    B.barrier()
    ph2b.close()
    ph2k.close()
    early_tm.close()

    if PH < 3:
        return finish()

    qan_fm = B.sbuf(early_tm, "qan_fm", [128, DQA // 128, TOK], BF16, side="right")
    ph3a = ExitStack()
    qaw_s = B.sbuf(ph3a, "qaw_s", [128, D // 128, DQA], BF16)
    for c in range(D // 128):
        dma("gpsimd", qaw_s[:, c, :], qaw_t.ap()[c*128:(c+1)*128, :], s_wl)
    v_wqa = s_wl.n
    if os.environ.get("MK_AGKV_EARLY", "0") == "1":
        wait("gpsimd", s_bn, s_bn.n)
        on("gpsimd", lambda e: e.collective_compute(
            "AllGather", OP.bypass, replica_groups=[list(range(NCORES))],
            ins=[kvb.ap().opt()], outs=[gkv.ap().opt()]), inc=s_cc)
    qa_ps = [B.psum(ph3a, "qa_ps%d" % i, [128, 1536], F32) for i in range(2)]
    tr_ps3 = [B.psum(ph3a, "tr3_%d" % i, [128, 1024], BF16) for i in range(2)]
    qan_tm = B.sbuf(ph3a, "qan_tm", [128, NT, DQA], BF16)
    wait("tensor", s_wl, v_wqa)
    evac_v = [0, 0]
    for t in range(NT):
        pp = qa_ps[t % 2]
        wait("tensor", s_vx, evac_v[t % 2])
        for c in range(D // 128):
            for nb in range(DQA // 512):
                on("tensor", lambda e, t=t, c=c, nb=nb, pp=pp: e.matmul(
                    pp[:, nb*512:(nb+1)*512], xln1_fm[:, c, t*128:(t+1)*128],
                    qaw_s[:, c, nb*512:(nb+1)*512],
                    start=(c == 0), stop=(c == D // 128 - 1)))
        mv = tbar()
        wait("scalar", s_mm, mv)
        emit_rms_tile(t, DQA, pp[:, :])
        on("vector", lambda e, t=t, pp=pp: e.tensor_scalar(
            qan_tm[:, t, :], pp[:, :], st_r[:, t:t+1], None, OP.mult))
        evac_v[t % 2] = vbar()
    v_qan = s_vx.n
    pairs = [(qan_tm[:, t, c*128:(c+1)*128], qan_fm[:, c, t*128:(t+1)*128], c)
             for t in range(NT) for c in range(DQA // 128)]
    emit_transposes(pairs, tr_ps3, gamma=qa_gs, wait_sem=s_vx, wait_val=v_qan)
    B.dbg_dump("qan_fm", qan_fm[:, :, :], [128, DQA // 128, TOK], BF16, s_out, waits=(s_vx, s_sc))
    B.barrier()
    ph3a.close()

    ph3b = ExitStack()
    NFQ = 6
    NSL = (H * QHD // 128) // NFQ
    qbw_sl = B.sbuf(ph3b, "qbw_sl", [128, 2, DQA // 128, NFQ * 128], BF16)
    qh_sb = B.sbuf(ph3b, "qh_sb", [128, 4, TOK], BF16)
    qpe_bf = B.sbuf(ph3b, "qpe_bf", [128, TOK], BF16)
    rope_s1 = B.sbuf(ph3b, "rope_s1", [128, TOK], F32)
    rope_s2 = B.sbuf(ph3b, "rope_s2", [128, TOK], F32)
    rotm_s = B.sbuf(ph3b, "rotm_s", [128, 128], BF16)
    q_ps = [B.psum(ph3b, "q_ps%d" % i, [128, 512], F32) for i in range(2)]
    rot_ps = B.psum(ph3b, "rot_ps", [128, 1024], F32)
    dma("gpsimd", rotm_s[:, :], rotm_in.ap(), s_wl)
    q_wm = []
    evac = [0, 0]
    q_dma_v = [0, 0, 0, 0]
    k = 0
    for fq in range(NSL):
        slot = fq % 2
        if fq >= 2:
            wait("gpsimd", s_mm, q_wm[fq - 2])
        for c in range(DQA // 128):
            dma("gpsimd", qbw_sl[:, slot, c, :],
                qbw_t.ap()[c*128:(c+1)*128, fq*NFQ*128:(fq+1)*NFQ*128], s_wl)
        v_w = s_wl.n
        wait("tensor", s_wl, v_w)
        for f6 in range(NFQ):
            f = fq * NFQ + f6
            hslot = f % 4
            if f < H:
                wait("vector", s_bn, q_dma_v[hslot])
            for tb in range(TOK // 512):
                pp = q_ps[k % 2]
                wait("tensor", s_vx, evac[k % 2])
                for c in range(DQA // 128):
                    on("tensor", lambda e, slot=slot, c=c, f6=f6, tb=tb, pp=pp: e.matmul(
                        pp[:, :], qbw_sl[:, slot, c, f6*128:(f6+1)*128],
                        qan_fm[:, c, tb*512:(tb+1)*512],
                        start=(c == 0), stop=(c == DQA // 128 - 1)))
                mv = tbar()
                wait("vector", s_mm, mv)
                if f < H:
                    on("vector", lambda e, hslot=hslot, tb=tb, pp=pp: e.tensor_scalar(
                        qh_sb[:, hslot, tb*512:(tb+1)*512], pp[:, :], SCALE, None, OP.mult))
                else:
                    on("vector", lambda e, tb=tb, pp=pp: e.tensor_scalar(
                        qpe_bf[:, tb*512:(tb+1)*512], pp[:, :], SCALE, None, OP.mult))
                evac[k % 2] = vbar()
                k += 1
            if f >= H:
                p = f - H
                v_pe = s_vx.n
                wait("tensor", s_vx, v_pe)
                for tb in range(2):
                    on("tensor", lambda e, tb=tb: e.matmul(
                        rot_ps[:, tb*512:(tb+1)*512], rotm_s[:, :],
                        qpe_bf[:, tb*512:(tb+1)*512], start=True, stop=True))
                mv = tbar()
                wait("vector", s_mm, mv)
                wait("vector", s_bn, q_dma_v[(p + H) % 4])
                on("vector", lambda e: e.tensor_tensor(
                    rope_s1[:, :], qpe_bf[:, :], cosfm[:, :], OP.mult))
                on("vector", lambda e: e.tensor_tensor(
                    rope_s2[:, :], rot_ps[:, :], sinfm[:, :], OP.mult))
                wait("vector", s_vx, vbar())
                on("vector", lambda e, p=p: e.tensor_tensor(
                    qh_sb[:, (p + H) % 4, :], rope_s1[:, :], rope_s2[:, :], OP.add))
            v_f = vbar()
            wait("sync", s_vx, v_f)
            if f < H:
                q_dma_v[hslot] = dma(
                    "sync", qbb.ap()[R_QN + f*128: R_QN + (f+1)*128, :],
                    qh_sb[:, hslot, :], s_bn)
            else:
                p = f - H
                q_dma_v[(p + H) % 4] = dma(
                    "sync", qbb.ap()[R_QPE + p*128: R_QPE + (p+1)*128, :],
                    qh_sb[:, (p + H) % 4, :], s_bn)
        q_wm.append(s_mm.n)
    B.barrier()
    ph3b.close()
    early_tm.close()
    ph1.close()
    early_fm.close()

    wait("gpsimd", s_bn, s_bn.n)
    if os.environ.get("MK_AGKV_EARLY", "0") != "1":
        on("gpsimd", lambda e: e.collective_compute(
            "AllGather", OP.bypass, replica_groups=[list(range(NCORES))],
            ins=[kvb.ap().opt()], outs=[gkv.ap().opt()]), inc=s_cc)
    on("gpsimd", lambda e: e.collective_compute(
        "AllGather", OP.bypass, replica_groups=[list(range(NCORES))],
        ins=[qbb.ap().opt()], outs=[gq.ap().opt()]), inc=s_cc)
    if "gbuf" in B.dbg:
        wait("sync", s_bn, s_bn.n)
        B.dbg_dump("b_kv", kvb.ap(), [KVROWS, TOK], BF16, s_out)
        B.dbg_dump("b_q", qbb.ap(), [QROWS, TOK], BF16, s_out)
    B.barrier()

    if PH < 4:
        return finish()

    ph4 = ExitStack()
    qnh = B.sbuf(ph4, "qnh", [128, T], BF16)
    qph = B.sbuf(ph4, "qph", [64, T], BF16)
    knh = B.sbuf(ph4, "knh", [128, T], BF16)
    kph = B.sbuf(ph4, "kph", [64, T], BF16)
    vfmh = B.sbuf(ph4, "vfmh", [128, T], BF16)
    vh = B.sbuf(ph4, "vh", [128, T // 128, VD], BF16)
    p_sb = B.sbuf(ph4, "p_sb", [128, T], BF16)
    pt_sb = B.sbuf(ph4, "pt_sb", [128, 512], BF16)
    at_scr = B.sbuf(ph4, "at_scr", [128, 128], BF16)
    afm = B.sbuf(ph4, "afm", [128, T], BF16)
    ssum = B.sbuf(ph4, "ssum", [128, 16], F32)
    nrow = B.sbuf(ph4, "nrow", [128, 1], F32)
    rcp = B.sbuf(ph4, "rcp", [128, 1], F32)
    ones_c = B.sbuf(ph4, "ones_c", [128, 1], BF16)
    onesr = B.sbuf(ph4, "onesr", [1, 128], BF16)
    nscr = B.sbuf(ph4, "nscr", [1, 4], F32)
    nscr_bf = B.sbuf(ph4, "nscr_bf", [1, 1], BF16)
    negb = B.sbuf(ph4, "negb", [128, 1], F32)
    nrm_ps = B.psum(ph4, "nrm_ps", [1, 512], F32)
    bc_ps = B.psum(ph4, "bc_ps", [128, 512], F32)
    on("vector", lambda e: e.memset(ones_c[:, :], 1.0))
    on("vector", lambda e: e.memset(onesr[:, :], 1.0))
    vbar()

    sc_ps = [B.psum(ph4, "sc_ps%d" % i, [128, 512], F32) for i in range(2)]
    tr_ps4 = [B.psum(ph4, "tr4_%d" % i, [128, 1024], BF16) for i in range(2)]
    pv_ps = [B.psum(ph4, "pv_ps%d" % i, [128, 512], F32) for i in range(2)]

    wait("sync", s_cc, 1)
    for r in range(NCORES):
        dma("sync", kph[0:ROPE, r*TOK:(r+1)*TOK],
            gkv.ap()[KVROWS*r + R_KPE: KVROWS*r + R_KPE + ROPE, :], s_ld)
    v_kpe = s_ld.n

    def attn_loads_kv(e, hd):
        with e.register("rkc%d" % hd) as r_c, e.register("rkh%d" % hd) as r_h, \
             e.register("rko%d" % hd) as r_o:
            e.reg_load(r_c, cid_s[0:1, 0:1])
            e.reg_alu(r_h, r_c, 2, OP.mult)
            e.reg_alu(r_h, r_h, hd, OP.add)
            RB = KVROWS * TOK
            e.reg_alu(r_o, r_h, 128 * TOK, OP.mult)
            e.reg_alu(r_o, r_o, R_KN * TOK, OP.add)
            e.dma_start(knh[:, :],
                        bass.AP(gkv, r_o, [[TOK, 128], [RB, NCORES], [1, TOK]])
                        ).then_inc(s_ld.h, 16)
            e.reg_alu(r_o, r_h, 128 * TOK, OP.mult)
            e.reg_alu(r_o, r_o, R_VF * TOK, OP.add)
            e.dma_start(vfmh[:, :],
                        bass.AP(gkv, r_o, [[TOK, 128], [RB, NCORES], [1, TOK]])
                        ).then_inc(s_ld.h, 16)

    def attn_loads_q(e, hd):
        with e.register("rqc%d" % hd) as r_c, e.register("rqh%d" % hd) as r_h, \
             e.register("rqo%d" % hd) as r_o:
            e.reg_load(r_c, cid_s[0:1, 0:1])
            e.reg_alu(r_h, r_c, 2, OP.mult)
            e.reg_alu(r_h, r_h, hd, OP.add)
            RB = QROWS * TOK
            e.reg_alu(r_o, r_h, 128 * TOK, OP.mult)
            e.reg_alu(r_o, r_o, R_QN * TOK, OP.add)
            e.dma_start(qnh[:, :],
                        bass.AP(gq, r_o, [[TOK, 128], [RB, NCORES], [1, TOK]])
                        ).then_inc(s_ld.h, 16)
            e.reg_alu(r_o, r_h, 64 * TOK, OP.mult)
            e.reg_alu(r_o, r_o, R_QPE * TOK, OP.add)
            e.dma_start(qph[0:64, :],
                        bass.AP(gq, r_o, [[TOK, 64], [RB, NCORES], [1, TOK]])
                        ).then_inc(s_ld.h, 16)

    ATT_QT = int(os.environ.get("MK_ATT_QT", str(T // 128)))
    ATT_HD = int(os.environ.get("MK_ATT_HD", "2"))
    exp_v = [0, 0]
    tr_done = [0, 0]
    pv_evac = [0, 0]
    for hd in range(ATT_HD):
        s_ld.n += 2 * 16
        on("sync", lambda e, hd=hd: attn_loads_kv(e, hd))
        v_kv = s_ld.n
        wait("sync", s_cc, 2)
        s_ld.n += 2 * 16
        on("sync", lambda e, hd=hd: attn_loads_q(e, hd))
        v_qkv = s_ld.n
        wait("tensor", s_ld, v_kv)
        if hd == 0:
            wait("tensor", s_ld, v_kpe)
        if hd == 0 and "att" in B.dbg:
            B.dbg_dump("att_gq", qnh[:, :], [128, T], BF16, s_out, waits=(s_ld,))
            B.dbg_dump("att_gk", knh[:, :], [128, T], BF16, s_out, waits=(s_ld,))
            B.dbg_dump("att_gqp", qph[0:64, :], [64, T], BF16, s_out, waits=(s_ld,))
            B.dbg_dump("att_gkp", kph[0:64, :], [64, T], BF16, s_out, waits=(s_ld,))
            B.dbg_dump("att_gvf", vfmh[:, :], [128, T], BF16, s_out, waits=(s_ld,))
        vtr = [0, 0]
        for s in range(T // 128):
            pp = tr_ps4[s % 2]
            wait("tensor", s_vx, vtr[s % 2])
            on("tensor", lambda e, s=s, pp=pp: e.transpose(
                pp[:, 0:128], vfmh[:, s*128:(s+1)*128], ident[:, :]))
            mv = tbar()
            wait("vector", s_mm, mv)
            on("vector", lambda e, s=s, pp=pp: e.tensor_copy(
                vh[:, s, :], pp[:, 0:128]))
            vtr[s % 2] = vbar()
        tr_done[0] = max(tr_done[0], vtr[0])
        tr_done[1] = max(tr_done[1], vtr[1])
        for which, buf, bufp in ((0, knh, kph), (1, qnh, qph)):
            if which == 0:
                wait("vector", s_ld, v_kv)
                if hd == 0:
                    wait("vector", s_ld, v_kpe)
            else:
                wait("vector", s_ld, v_qkv)
                wait("tensor", s_ld, v_qkv)
            on("vector", lambda e, buf=buf: e.tensor_tensor(
                p_sb[:, :], buf[:, :], buf[:, :], OP.mult))
            wait("vector", s_vx, vbar())
            wait("tensor", s_vx, s_vx.n)
            for cb in range(T // 512):
                on("tensor", lambda e, cb=cb: e.matmul(
                    nrm_ps[0:1, :], ones_c[:, 0:1], p_sb[:, cb*512:(cb+1)*512],
                    start=True, stop=False, skip_group_check=True))
            on("vector", lambda e, bufp=bufp: e.tensor_tensor(
                p_sb[0:64, :], bufp[0:64, :], bufp[0:64, :], OP.mult))
            wait("vector", s_vx, vbar())
            wait("tensor", s_vx, s_vx.n)
            for cb in range(T // 512):
                on("tensor", lambda e, cb=cb: e.matmul(
                    nrm_ps[0:1, :], ones_c[0:64, 0:1], p_sb[0:64, cb*512:(cb+1)*512],
                    start=False, stop=(cb == T // 512 - 1), skip_group_check=True))
            mv = tbar()
            wait("vector", s_mm, mv)
            on("vector", lambda e, which=which: e.reduce_max(
                nscr[0:1, which:which+1], nrm_ps[0:1, 0:512], axis=AX))
            wait("vector", s_vx, vbar())
        wait("scalar", s_vx, s_vx.n)
        on("scalar", lambda e: e.activation(
            nscr[0:1, 0:2], nscr[0:1, 0:2], AF.Sqrt), inc=s_sc)
        wait("vector", s_sc, s_sc.n)
        on("vector", lambda e: e.tensor_tensor(
            nscr[0:1, 2:3], nscr[0:1, 0:1], nscr[0:1, 1:2], OP.mult))
        wait("vector", s_vx, vbar())
        on("vector", lambda e: e.tensor_scalar(
            nscr[0:1, 3:4], nscr[0:1, 2:3], -1.05, -1.0, OP.mult, OP.add))
        wait("vector", s_vx, vbar())
        on("vector", lambda e: e.tensor_copy(nscr_bf[0:1, 0:1], nscr[0:1, 3:4]))
        v_nb = vbar()
        wait("tensor", s_vx, v_nb)
        on("tensor", lambda e: e.matmul(
            bc_ps[:, 0:1], onesr[0:1, :], nscr_bf[0:1, 0:1], start=True, stop=True))
        mv = tbar()
        wait("vector", s_mm, mv)
        on("vector", lambda e: e.tensor_copy(negb[:, :], bc_ps[:, 0:1]))
        v_negb = vbar()
        wait("scalar", s_vx, v_negb)
        for qt in range(ATT_QT):
            KB = qt // 4 + 1
            exp_of_kb = {}
            for kb in range(KB):
                ns = 4 if kb < KB - 1 else (qt - 4 * (KB - 1) + 1)
                N = 128 * ns
                pp = sc_ps[kb % 2]
                wait("tensor", s_sc, exp_v[kb % 2])
                on("tensor", lambda e, qt=qt, kb=kb, N=N, pp=pp: e.matmul(
                    pp[:, 0:N], qnh[:, qt*128:(qt+1)*128],
                    knh[:, kb*512: kb*512+N], start=True, stop=False))
                on("tensor", lambda e, qt=qt, kb=kb, N=N, pp=pp: e.matmul(
                    pp[:, 0:N], qph[0:64, qt*128:(qt+1)*128],
                    kph[0:64, kb*512: kb*512+N], start=False, stop=True))
                mv = tbar()
                if kb == KB - 1:
                    wait("vector", s_mm, mv)
                    on("vector", lambda e, N=N, pp=pp: e.tensor_tensor(
                        pp[:, N-128:N], pp[:, N-128:N], mask[:, :], OP.add))
                    wait("scalar", s_vx, vbar())
                else:
                    wait("scalar", s_mm, mv)
                exp_v[kb % 2] = on("scalar", lambda e, kb=kb, N=N, pp=pp: e.activation(
                    p_sb[:, kb*512: kb*512+N], pp[:, 0:N], AF.Exp,
                    bias=negb[:, 0:1], accum_out=ssum[:, kb:kb+1]), inc=s_sc)
                exp_of_kb[kb] = exp_v[kb % 2]
            v_exp_all = s_sc.n
            wait("vector", s_sc, v_exp_all)
            on("vector", lambda e, KB=KB: e.reduce_sum(
                nrow[:, 0:1], ssum[:, 0:KB], axis=AX))
            wait("vector", s_vx, vbar())
            on("vector", lambda e: e.reciprocal(rcp[:, 0:1], nrow[:, 0:1]))
            v_rcp = vbar()
            wait("tensor", s_vx, pv_evac[qt % 2])
            for kb in range(KB):
                ns = 4 if kb < KB - 1 else (qt - 4 * (KB - 1) + 1)
                N = 128 * ns
                pp = tr_ps4[kb % 2]
                wait("tensor", s_vx, tr_done[kb % 2])
                wait("tensor", s_sc, exp_of_kb[kb])
                for s in range(ns):
                    on("tensor", lambda e, kb=kb, s=s, pp=pp: e.transpose(
                        pp[:, s*128:(s+1)*128], p_sb[:, (kb*4+s)*128:(kb*4+s+1)*128],
                        ident[:, :]))
                mv = tbar()
                wait("vector", s_mm, mv)
                on("vector", lambda e, N=N, pp=pp: e.tensor_copy(
                    pt_sb[:, 0:N], pp[:, 0:N]))
                tr_done[kb % 2] = vbar()
                wait("tensor", s_vx, tr_done[kb % 2])
                for s in range(ns):
                    on("tensor", lambda e, qt=qt, kb=kb, s=s, KB=KB, ns=ns: e.matmul(
                        pv_ps[qt % 2][:, 0:VD], pt_sb[:, s*128:(s+1)*128],
                        vh[:, kb*4+s, :],
                        start=(kb == 0 and s == 0),
                        stop=(kb == KB - 1 and s == ns - 1),
                        skip_group_check=True))
            mv = tbar()
            wait("vector", s_mm, mv)
            wait("vector", s_vx, v_rcp)
            on("vector", lambda e, qt=qt: e.tensor_scalar(
                at_scr[:, :], pv_ps[qt % 2][:, 0:VD], rcp[:, 0:1], None, OP.mult))
            pv_evac[qt % 2] = vbar()
            ppt = tr_ps4[qt % 2]
            wait("tensor", s_vx, tr_done[qt % 2])
            wait("tensor", s_vx, pv_evac[qt % 2])
            on("tensor", lambda e, ppt=ppt: e.transpose(
                ppt[:, 0:128], at_scr[:, :], ident[:, :]))
            mv = tbar()
            wait("vector", s_mm, mv)
            on("vector", lambda e, qt=qt, ppt=ppt: e.tensor_copy(
                afm[:, qt*128:(qt+1)*128], ppt[:, 0:128]))
            tr_done[qt % 2] = vbar()
            if hd == 0 and qt in (0, 5) and "att" in B.dbg:
                pw = (qt // 4) * 512 + (qt % 4 + 1) * 128
                B.dbg_dump("att_p%d" % qt, p_sb[:, 0:pw], [128, pw], BF16, s_out, waits=(s_vx, s_sc))
                B.dbg_dump("att_negm%d" % qt, negm[:, 0:qt//4+1], [128, qt//4+1], F32, s_out, waits=(s_vx, s_sc))
                B.dbg_dump("att_ssum%d" % qt, ssum[:, 0:qt//4+1], [128, qt//4+1], F32, s_out, waits=(s_vx, s_sc))
                B.dbg_dump("att_alph%d" % qt, alph[:, 0:qt//4+1], [128, qt//4+1], F32, s_out, waits=(s_vx, s_sc))
                B.dbg_dump("att_nrow%d" % qt, nrow[:, :], [128, 1], F32, s_out, waits=(s_vx, s_sc))
                B.dbg_dump("att_rcp%d" % qt, rcp[:, :], [128, 1], F32, s_out, waits=(s_vx, s_sc))
                B.dbg_dump("att_scr%d" % qt, at_scr[:, :], [128, 128], BF16, s_out, waits=(s_vx, s_sc))
                B.dbg_dump("att_qnh%d" % qt, qnh[:, 0:1024], [128, 1024], BF16, s_out, waits=(s_vx, s_sc))
                B.dbg_dump("att_knh%d" % qt, knh[:, 0:1024], [128, 1024], BF16, s_out, waits=(s_vx, s_sc))
                B.dbg_dump("att_vh%d" % qt, vh[:, 0:8, :], [128, 8, VD], BF16, s_out, waits=(s_vx, s_sc))
        if hd == 0 and ATT_QT == T // 128 and "att" in B.dbg:
            B.dbg_dump("att_afm", afm[:, :], [128, T], BF16, s_out, waits=(s_vx, s_sc))
        if ATT_QT == T // 128:
            wait("sync", s_vx, s_vx.n)
            ab = abn0 if hd == 0 else abn1
            ga = ga0 if hd == 0 else ga1
            dma("sync", ab.ap()[:, :], afm[:, :], s_bn)
            wait("sync", s_bn, s_bn.n)
            wait("gpsimd", s_bn, s_bn.n)
            on("gpsimd", lambda e, ab=ab, ga=ga: e.collective_compute(
                "AllGather", OP.bypass, replica_groups=[list(range(NCORES))],
                ins=[ab.ap().opt()], outs=[ga.ap().opt()]), inc=s_cc)
    B.barrier()
    ph4.close()

    if PH < 5:
        return finish()

    ph5 = ExitStack()
    at_own = B.sbuf(ph5, "at_own", [128, 2, H // 2, TOK], BF16)
    ow_s = B.sbuf(ph5, "ow_s", [128, H, D], BF16)
    op_ps = [B.psum(ph5, "op_ps%d" % i, [128, 2048], F32) for i in range(2)]
    for c in range(H):
        dma("gpsimd", ow_s[:, c, :], ow_t.ap()[c*128:(c+1)*128, :], s_wl)
    v_wo = s_wl.n

    def attn_own_loads(e, par, ga):
        with e.register("roc%d" % par) as r_c, e.register("roo%d" % par) as r_o:
            e.reg_load(r_c, cid_s[0:1, 0:1])
            e.reg_alu(r_o, r_c, TOK, OP.mult)
            e.dma_start(at_own[:, par, :, :],
                        bass.AP(ga, r_o, [[T, 128], [VD * T, H // 2], [1, TOK]])
                        ).then_inc(s_ld.h, 16)
    if "gbuf" in B.dbg:
        wait("sync", s_cc, s_cc.n)
        B.dbg_dump("g_kv", gkv.ap(), [NCORES * KVROWS, TOK], BF16, s_out)
        B.dbg_dump("g_q", gq.ap(), [NCORES * QROWS, TOK], BF16, s_out)
        B.dbg_dump("g_a0", ga0.ap(), [NCORES * VD, T], BF16, s_out)
    wait("sync", s_cc, s_cc.n - 1)
    s_ld.n += 16
    on("sync", lambda e: attn_own_loads(e, 0, ga0))
    v_at_e = s_ld.n
    wait("sync", s_cc, s_cc.n)
    s_ld.n += 16
    on("sync", lambda e: attn_own_loads(e, 1, ga1))
    v_at_o = s_ld.n
    wait("tensor", s_ld, v_at_e)
    wait("tensor", s_wl, v_wo)
    res_v = [0, 0]
    for t in range(NT):
        pp = op_ps[t % 2]
        wait("tensor", s_vx, res_v[t % 2])
        order = [(0, k) for k in range(H // 2)] + [(1, k) for k in range(H // 2)]
        for j, (par, k) in enumerate(order):
            if t == 0 and par == 1 and k == 0:
                wait("tensor", s_ld, v_at_o)
            for nb in range(4):
                on("tensor", lambda e, t=t, par=par, k=k, nb=nb, pp=pp, j=j: e.matmul(
                    pp[:, nb*512:(nb+1)*512], at_own[:, par, k, t*128:(t+1)*128],
                    ow_s[:, 2*k + par, nb*512:(nb+1)*512],
                    start=(j == 0), stop=(j == H - 1)))
        mv = tbar()
        wait("vector", s_mm, mv)
        on("vector", lambda e, t=t, pp=pp: e.tensor_tensor(
            x_tm[:, t, :], x_tm[:, t, :], pp[:, :], OP.add))
        res_v[t % 2] = vbar()
    B.barrier()
    ph5.close()

    if PH < 6:
        return finish()

    ph6 = ExitStack()
    xln2_fm = B.sbuf(ph6, "xln2_fm", [128, D // 128, TOK], BF16)
    cw_sb = B.sbuf(ph6, "cw_sb", [128, NT, NCH, 2], F32)
    cws_sb = B.sbuf(ph6, "cws_sb", [128, NT, NCH], F32)
    ph6b = ExitStack()
    ln2_scr = B.sbuf(ph6b, "ln2_scr", [128, NT, D], BF16)
    tr_ps6 = [B.psum(ph6b, "tr6_%d" % i, [128, 1024], BF16) for i in range(2)]
    v_stat2 = emit_norm_stats(lambda t: x_tm[:, t, :], NT, D)
    wait("vector", s_vx, v_stat2)
    for t in range(NT):
        on("vector", lambda e, t=t: e.tensor_scalar(
            ln2_scr[:, t, :], x_tm[:, t, :], st_m[:, t:t+1], st_r[:, t:t+1],
            OP.subtract, OP.mult))
    v_ln2 = vbar()
    pairs = [(ln2_scr[:, t, c*128:(c+1)*128], xln2_fm[:, c, t*128:(t+1)*128], c)
             for t in range(NT) for c in range(D // 128)]
    emit_transposes(pairs, tr_ps6, wait_sem=s_vx, wait_val=v_ln2)
    B.dbg_dump("xln2_fm", xln2_fm[:, :, :], [128, D // 128, TOK], BF16, s_out, waits=(s_vx, s_sc))
    B.barrier()
    ph6b.close()

    if PH < 7:
        return finish()

    ph7 = ExitStack()
    gate_s = B.sbuf(ph7, "gate_s", [128, H, NE], BF16)
    riota = B.sbuf(ph7, "riota", [128, NE], F32)
    lg_sb = B.sbuf(ph7, "lg_sb", [128, NT, NE], F32)
    exp_sb = B.sbuf(ph7, "exp_sb", [128, NT, NE], F32)
    sig_sb = B.sbuf(ph7, "sig_sb", [128, NT, NE], F32)
    sel_sb = B.sbuf(ph7, "sel_sb", [128, NT, NE], F32)
    r_t = B.sbuf(ph7, "r_t", [128, NE], F32)
    r_u = B.sbuf(ph7, "r_u", [128, NE], F32)
    r_m = B.sbuf(ph7, "r_m", [128, 1], F32)
    lg_ps = [B.psum(ph7, "lg_ps%d" % i, [128, 512], F32) for i in range(2)]
    for c in range(H):
        dma("gpsimd", gate_s[:, c, :], gate_t.ap()[c*128:(c+1)*128, :], s_wl)
    dma("sync", riota[:, :], riota_in.ap(), s_ld)
    v_wg = s_wl.n
    v_ri = s_ld.n
    wait("tensor", s_wl, v_wg)
    evac = [0, 0]
    for t in range(NT):
        pp = lg_ps[t % 2]
        wait("tensor", s_vx, evac[t % 2])
        for c in range(H):
            on("tensor", lambda e, t=t, c=c, pp=pp: e.matmul(
                pp[:, 0:NE], xln2_fm[:, c, t*128:(t+1)*128], gate_s[:, c, :],
                start=(c == 0), stop=(c == H - 1)))
        mv = tbar()
        wait("vector", s_mm, mv)
        on("vector", lambda e, t=t, pp=pp: e.tensor_copy(lg_sb[:, t, :], pp[:, 0:NE]))
        evac[t % 2] = vbar()
    v_lg = s_vx.n
    wait("vector", s_vx, v_lg)
    for t in range(NT):
        on("vector", lambda e, t=t: e.reduce_max(
            st_q[:, t:t+1], lg_sb[:, t, :], axis=AX, negate=True))
    vv = vbar()
    wait("scalar", s_vx, vv)
    for t in range(NT):
        on("scalar", lambda e, t=t: e.activation(
            exp_sb[:, t, :], lg_sb[:, t, :], AF.Exp, bias=st_q[:, t:t+1]))
    for t in range(NT):
        on("scalar", lambda e, t=t: e.activation(
            sig_sb[:, t, :], lg_sb[:, t, :], AF.Sigmoid), inc=s_sc)
    v_sg = s_sc.n
    wait("vector", s_sc, v_sg)
    wait("vector", s_ld, v_ri)
    on("vector", lambda e: e.memset(sel_sb[:, :, :], 0.0))
    for t in range(NT):
        on("vector", lambda e, t=t: e.tensor_copy(r_t[:, :], sig_sb[:, t, :]))
        wait("vector", s_vx, vbar())
        for kk in range(4):
            on("vector", lambda e: e.reduce_max(r_m[:, 0:1], r_t[:, :], axis=AX))
            wait("vector", s_vx, vbar())
            on("vector", lambda e: e.tensor_scalar(
                r_u[:, :], r_t[:, :], r_m[:, 0:1], None, OP.is_equal))
            wait("vector", s_vx, vbar())
            on("vector", lambda e: e.tensor_tensor(r_u[:, :], r_u[:, :], riota[:, :], OP.mult))
            wait("vector", s_vx, vbar())
            on("vector", lambda e: e.reduce_max(r_m[:, 0:1], r_u[:, :], axis=AX))
            wait("vector", s_vx, vbar())
            on("vector", lambda e: e.tensor_scalar(
                r_u[:, :], r_u[:, :], r_m[:, 0:1], None, OP.is_equal))
            wait("vector", s_vx, vbar())
            on("vector", lambda e, t=t: e.tensor_tensor(
                sel_sb[:, t, :], sel_sb[:, t, :], r_u[:, :], OP.add))
            on("vector", lambda e: e.scalar_tensor_tensor(
                r_t[:, :], r_u[:, :], -2.0, r_t[:, :], OP.mult, OP.add))
            wait("vector", s_vx, vbar())
    for t in range(NT):
        on("vector", lambda e, t=t: e.tensor_tensor(
            sel_sb[:, t, :], exp_sb[:, t, :], sel_sb[:, t, :], OP.mult))
        wait("vector", s_vx, vbar())
        on("vector", lambda e, t=t: e.reduce_sum(st_q[:, t:t+1], sel_sb[:, t, :], axis=AX))
    vv = vbar()
    wait("vector", s_vx, vv)
    on("vector", lambda e: e.reciprocal(st_r[:, 0:NT], st_q[:, 0:NT]))
    vv = vbar()
    wait("vector", s_vx, vv)
    for t in range(NT):
        on("vector", lambda e, t=t: e.tensor_scalar(
            cw_sb[:, t, :, :], sel_sb[:, t, :], st_r[:, t:t+1], None, OP.mult))
        wait("vector", s_vx, vbar())
        on("vector", lambda e, t=t: e.tensor_tensor(
            cws_sb[:, t, :], cw_sb[:, t, :, 0:1], cw_sb[:, t, :, 1:2], OP.add))
        wait("vector", s_vx, vbar())
        on("vector", lambda e, t=t: e.tensor_scalar(
            cws_sb[:, t, :], cws_sb[:, t, :], SMALL_SCALE, None, OP.mult))
    v_cw = vbar()
    B.dbg_dump("cw", cw_sb[:, :, :, :], [128, NT, NCH, 2], F32, s_out, waits=(s_vx, s_sc))
    B.dbg_dump("cws", cws_sb[:, :, :], [128, NT, NCH], F32, s_out, waits=(s_vx, s_sc))
    B.barrier()
    ph7.close()

    if PH < 8:
        return finish()

    ph8 = ExitStack()
    wgsl = B.sbuf(ph8, "wgsl", [128, 2, H, 256], BF16)
    wusl = B.sbuf(ph8, "wusl", [128, 2, H, 256], BF16)
    wdsl = B.sbuf(ph8, "wdsl", [128, 2, DI // 128, D], BF16)
    a_sb = B.sbuf(ph8, "a_sb", [128, DI // 128, 2, 512], BF16)
    asil = B.sbuf(ph8, "asil", [128, 512], F32)
    gu_ps = [B.psum(ph8, "gu_ps%d" % i, [128, 512], F32) for i in range(4)]
    pd_ps = [B.psum(ph8, "pd_ps%d" % i, [128, 1024], F32) for i in range(2)]

    wait("vector", s_vx, v_cw)
    st = {"w_use": 0, "w_wm": [], "d_wm": [], "gu": 0,
          "gu_evac": [0, 0], "pd_evac": [0, 0], "a_consumed": 0,
          "sil_free": 0}

    def moe_expert(e_i, n_i, nhalf, g_src, u_src, d_src, cw_slice):
        dslot = e_i % 2
        if len(st["d_wm"]) >= 2:
            wait("gpsimd", s_mm, st["d_wm"][-2])
        dma("gpsimd", wdsl[:, dslot, 0:n_i, :], d_src(e_i), s_wl)
        v_wd = s_wl.n
        n_per_half = n_i // nhalf
        for ihalf in range(nhalf):
            wslot = st["w_use"] % 2
            if st["w_use"] >= 2:
                wait("gpsimd", s_mm, st["w_wm"][st["w_use"] - 2])
            wdc = 128 * n_per_half
            dma("gpsimd", wgsl[:, wslot, :, 0:wdc], g_src(e_i, ihalf), s_wl)
            dma("gpsimd", wusl[:, wslot, :, 0:wdc], u_src(e_i, ihalf), s_wl)
            st["w_use"] += 1
            v_w = s_wl.n
            wait("tensor", s_wl, v_w)
            for i3 in range(n_per_half):
                i = ihalf * n_per_half + i3
                for tb in range(2):
                    par = st["gu"] % 2
                    pg, pu = gu_ps[2 * par], gu_ps[2 * par + 1]
                    wait("tensor", s_vx, st["gu_evac"][par])
                    for hh in range(H):
                        on("tensor", lambda e, wslot=wslot, i3=i3, hh=hh, tb=tb, pg=pg: e.matmul(
                            pg[:, :], wgsl[:, wslot, hh, i3*128:(i3+1)*128],
                            xln2_fm[:, hh, tb*512:(tb+1)*512],
                            start=(hh == 0), stop=(hh == H - 1)))
                    for hh in range(H):
                        on("tensor", lambda e, wslot=wslot, i3=i3, hh=hh, tb=tb, pu=pu: e.matmul(
                            pu[:, :], wusl[:, wslot, hh, i3*128:(i3+1)*128],
                            xln2_fm[:, hh, tb*512:(tb+1)*512],
                            start=(hh == 0), stop=(hh == H - 1)))
                    mv = tbar()
                    wait("scalar", s_mm, mv)
                    wait("scalar", s_vx, st["sil_free"])
                    sv = on("scalar", lambda e, pg=pg: e.activation(
                        asil[:, :], pg[:, :], AF.Silu), inc=s_sc)
                    wait("vector", s_sc, sv)
                    wait("vector", s_mm, st["a_consumed"])
                    on("vector", lambda e, i=i, tb=tb, pu=pu: e.tensor_tensor(
                        a_sb[:, i, tb, :], asil[:, :], pu[:, :], OP.mult))
                    st["gu_evac"][par] = vbar()
                    st["sil_free"] = st["gu_evac"][par]
                    st["gu"] += 1
            st["w_wm"].append(s_mm.n)
        v_a = s_vx.n
        wait("tensor", s_vx, v_a)
        wait("tensor", s_wl, v_wd)
        for tb in range(2):
            for t4 in range(4):
                t_glob = tb * 4 + t4
                for half in range(2):
                    php = pd_ps[half]
                    wait("tensor", s_vx, st["pd_evac"][half])
                    for i in range(n_i):
                        for nb in range(2):
                            on("tensor", lambda e, i=i, t4=t4, tb=tb, half=half, nb=nb, php=php: e.matmul(
                                php[:, nb*512:(nb+1)*512],
                                a_sb[:, i, tb, t4*128:(t4+1)*128],
                                wdsl[:, dslot, i, half*1024 + nb*512: half*1024 + (nb+1)*512],
                                start=(i == 0), stop=(i == n_i - 1)))
                    mv = tbar()
                    wait("vector", s_mm, mv)
                    on("vector", lambda e, t_glob=t_glob, half=half, php=php,
                            cw=cw_slice(e_i, t_glob): e.scalar_tensor_tensor(
                        x_tm[:, t_glob, half*1024:(half+1)*1024], php[:, :], cw,
                        x_tm[:, t_glob, half*1024:(half+1)*1024], OP.mult, OP.add))
                    st["pd_evac"][half] = vbar()
        st["a_consumed"] = s_mm.n
        st["d_wm"].append(s_mm.n)

    rr = lambda w: w.ap().rearrange("e (h p) c -> e p h c", p=128)
    g_src = lambda e_i, ihalf: rr(wg_w)[e_i, :, :, ihalf*256:(ihalf+1)*256]
    u_src = lambda e_i, ihalf: rr(wu_w)[e_i, :, :, ihalf*256:(ihalf+1)*256]
    d_src = lambda e_i: wd_w.ap().rearrange("e (i p) d -> e p i d", p=128)[e_i, :, :, :]
    cw_slice = lambda e_i, t: cw_sb[:, t, e_i // 2, e_i % 2: e_i % 2 + 1]
    for e_i in range(NE):
        moe_expert(e_i, DI // 128, 3, g_src, u_src, d_src, cw_slice)

    cg_src = lambda e_i, ihalf: rr(cwg_w)[e_i, :, :, :]
    cu_src = lambda e_i, ihalf: rr(cwu_w)[e_i, :, :, :]
    cd_src = lambda e_i: cwd_w.ap().rearrange("e (i p) d -> e p i d", p=128)[e_i, :, :, :]
    cws_slice = lambda e_i, t: cws_sb[:, t, e_i:e_i+1]
    for e_i in range(NCH):
        moe_expert(e_i, 1, 1, cg_src, cu_src, cd_src, cws_slice)
    B.barrier()
    ph8.close()

    return finish()



def _host_prep(inputs):
    f32 = np.float32
    x = np.asarray(inputs["L0_attn_hidden_states"], f32).reshape(T, D)
    q_a_w = np.asarray(inputs["L0_attn_q_a_w"], f32)
    q_a_ln = np.asarray(inputs["L0_attn_q_a_ln_w"], f32)
    q_b_w = np.asarray(inputs["L0_attn_q_b_w"], f32)
    kv_a_w = np.asarray(inputs["L0_attn_kv_a_w"], f32)
    kv_a_ln = np.asarray(inputs["L0_attn_kv_a_ln_w"], f32)
    kv_b_w = np.asarray(inputs["L0_attn_kv_b_w"], f32)
    o_w = np.asarray(inputs["L0_attn_o_w"], f32)
    gate_w = np.asarray(inputs["L0_ffn_gate_w"], f32)

    qaw_t = np.ascontiguousarray(q_a_w.T)
    qb = q_b_w.reshape(H, QHD, DQA)
    qb_nope = qb[:, :NOPE, :].reshape(H * NOPE, DQA)
    qb_pe = qb[:, NOPE:, :].reshape(H * ROPE, DQA)
    qbw_t = np.ascontiguousarray(np.concatenate([qb_nope, qb_pe], 0).T)
    qa_gamma = np.ascontiguousarray(q_a_ln.reshape(DQA // 128, 128).T)
    kvaw_t = np.ascontiguousarray(kv_a_w.T)
    kvb = kv_b_w.reshape(H, NOPE + VD, DCKV)
    kvb_n = kvb[:, :NOPE, :].reshape(H * NOPE, DCKV)
    kvb_v = kvb[:, NOPE:, :].reshape(H * VD, DCKV)
    kvbw_t = np.ascontiguousarray(np.concatenate([kvb_n, kvb_v], 0).T)
    kva_gamma = np.ascontiguousarray(kv_a_ln.reshape(DCKV // 128, 128).T)
    ow_t = np.ascontiguousarray(o_w.T)
    gate_t = np.ascontiguousarray(gate_w.T)

    ident = np.eye(128, dtype=ml_dtypes.bfloat16)
    masktile = np.triu(np.full((128, 128), NEGBIG, f32), k=1)
    inv_freq = 1.0 / (10000.0 ** (np.arange(0, ROPE, 2, dtype=f32) / ROPE))
    tpos = np.arange(T, dtype=f32)
    freqs = np.outer(tpos, inv_freq)
    emb = np.concatenate([freqs, freqs], -1)
    cos = np.cos(emb).astype(f32)
    sin = np.sin(emb).astype(f32)
    sin_eff = sin.copy()
    sin_eff[:, :ROPE // 2] *= -1.0
    rotm = np.zeros((128, 128), np.float32)
    for blk in range(2):
        b = blk * ROPE
        for r in range(ROPE // 2):
            rotm[b + r + ROPE // 2, b + r] = -1.0
            rotm[b + r, b + r + ROPE // 2] = 1.0
    rotm = rotm.astype(ml_dtypes.bfloat16)
    riota = np.ascontiguousarray(
        np.broadcast_to(np.arange(NE, 0, -1, dtype=f32), (128, NE)))

    shared = dict(
        qaw_t=qaw_t, qbw_t=qbw_t, qa_gamma=qa_gamma, kvaw_t=kvaw_t,
        kvbw_t=kvbw_t, kva_gamma=kva_gamma, ow_t=ow_t, gate_t=gate_t,
        Wg=np.ascontiguousarray(np.asarray(inputs["L0_ffn_Wg"], f32)),
        Wu=np.ascontiguousarray(np.asarray(inputs["L0_ffn_Wu"], f32)),
        Wd=np.ascontiguousarray(np.asarray(inputs["L0_ffn_Wd"], f32)),
        cWg=np.ascontiguousarray(np.asarray(inputs["L0_ffn_cWg"], f32)),
        cWu=np.ascontiguousarray(np.asarray(inputs["L0_ffn_cWu"], f32)),
        cWd=np.ascontiguousarray(np.asarray(inputs["L0_ffn_cWd"], f32)),
        ident_c=ident, masktile=masktile, riota_c=riota,
        epsc=np.ascontiguousarray(np.broadcast_to(np.array([EPS_LN, EPS_RMS], f32), (128, 2))),
        rotm=rotm,
    )
    per_core = []
    for j in range(NCORES):
        sl = slice(j * TOK, (j + 1) * TOK)
        cos_sl = cos[sl].T
        sin_sl = sin_eff[sl].T
        m = dict(
            x_own=np.ascontiguousarray(x[sl]),
            cosfm_c=np.ascontiguousarray(np.concatenate([cos_sl, cos_sl], 0)),
            sinfm_c=np.ascontiguousarray(np.concatenate([sin[sl].T, sin[sl].T], 0)),
            costm_c=np.ascontiguousarray(cos[sl]),
            sintm_c=np.ascontiguousarray(sin_eff[sl]),
            cid=np.array([[j]], np.int32),
        )
        m.update(shared)
        per_core.append(m)
    return per_core


def kernel(**inputs):
    nc, dbg_outs = build_program()
    maps = _host_prep(inputs)
    res = run_bass_kernel_spmd(nc, maps, core_ids=list(range(NCORES)))
    outs = [np.asarray(r["y"], np.float32) for r in res.results]
    y = np.concatenate(outs, 0).reshape(1, T, D)
    if dbg_outs:
        kernel.debug = [{k: r["dbg_" + k] for k in dbg_outs} for r in res.results]
    return y
